# version 17
# baseline (speedup 1.0000x reference)
import contextlib
import numpy as np
import ml_dtypes
import concourse.bass as bass
import concourse.mybir as mybir
from concourse.bass_utils import run_bass_kernel_spmd

F32 = mybir.dt.float32
BF16 = mybir.dt.bfloat16
AF = mybir.ActivationFunctionType
ALU = mybir.AluOpType
AX = mybir.AxisListType.X

B, T, D = 8, 2048, 2048
H, HD, C = 16, 128, 128
NCH, KC = T // C, D // 128
E, FF = 16, 512
G, NG = 4, 4
LN_EPS, RMS_EPS = 1e-5, 1e-6
ROPE_BASE = 10000.0
CTR = (C - 1) / 2.0
VW = 8

ENGS = ("tensor", "vector", "scalar", "gpsimd", "sync")


def _lam():
    h = np.arange(H, dtype=np.float64)
    return -(2.0 ** (-8.0 * (h + 1.0) / H))


def host_constants():
    lam = _lam()
    tcv = np.arange(C, dtype=np.float64)
    cst = {}
    j = np.arange(HD // 2, dtype=np.float64)
    theta = ROPE_BASE ** (-2.0 * j / HD)
    pos = np.arange(T, dtype=np.float64).reshape(NCH, C)
    ang = pos[:, :, None] * theta[None, None, :]
    cst["trig"] = np.concatenate(
        [np.cos(ang).transpose(1, 0, 2), np.sin(ang).transpose(1, 0, 2)], axis=2
    ).astype(np.float32)
    dk = np.exp(lam[None, :] * (C - 1.0 - tcv[:, None]))
    dk2 = np.exp(lam[None, :] * (CTR - tcv[:, None]))
    dqb = np.exp(lam[None, :] * (tcv[:, None] + 1.0 + CTR))
    dq2b = np.exp(lam[None, :] * (tcv[:, None] - CTR))
    cst["dcols"] = np.concatenate(
        [dk, dk2, dqb, dq2b, 0.5 * dq2b], axis=1).astype(np.float32)
    m1 = np.triu(np.ones((C, C), np.float32))
    cst["mask"] = m1
    cst["identb"] = np.eye(128, dtype=ml_dtypes.bfloat16)
    cst["identf"] = np.eye(128, dtype=np.float32)
    epsc = np.zeros((128, 4), np.float32)
    epsc[:, 0] = LN_EPS
    epsc[:, 1] = RMS_EPS
    cst["epsc"] = epsc
    cst["ones_row"] = np.ones((1, 128), np.float32)
    return cst


def host_relayout(inp):
    f32 = np.float32
    out = {}
    for nm, w in (("wq", inp["L0_attn_Wq"]), ("wk", inp["L0_attn_Wk"]),
                  ("wv", inp["L0_attn_Wv"])):
        out[nm] = np.ascontiguousarray(
            np.asarray(w, f32).reshape(KC, 128, D).transpose(1, 0, 2))
    out["wg1"] = np.ascontiguousarray(
        np.asarray(inp["L0_attn_Wg1"], f32).reshape(KC, 128, 128).transpose(1, 0, 2))
    out["wg2"] = np.ascontiguousarray(np.asarray(inp["L0_attn_Wg2"], f32))
    rw = np.concatenate([np.asarray(inp["L0_ffn_gate_w"], f32).T,
                         np.asarray(inp["L0_ffn_shared_gate_w"], f32)], axis=1)
    out["rw"] = np.ascontiguousarray(rw.reshape(KC, 128, 17).transpose(1, 0, 2))
    for nm, w in (("weg", inp["L0_ffn_Wg"]), ("weu", inp["L0_ffn_Wu"])):
        w = np.asarray(w, f32)
        out[nm] = np.ascontiguousarray(
            w.reshape(E, KC, 128, 4, 128).transpose(0, 3, 2, 1, 4))
    for nm, w in (("wsg", inp["L0_ffn_sWg"]), ("wsu", inp["L0_ffn_sWu"])):
        w = np.asarray(w, f32)
        out[nm] = np.ascontiguousarray(w.reshape(KC, 128, 4, 128).transpose(2, 1, 0, 3))
    wd = np.asarray(inp["L0_ffn_Wd"], f32)
    out["wed"] = np.ascontiguousarray(
        wd.reshape(E, 4, 128, 4, 512).transpose(0, 3, 2, 1, 4))
    wsd = np.asarray(inp["L0_ffn_sWd"], f32)
    out["wsd"] = np.ascontiguousarray(wsd.reshape(4, 128, 4, 512).transpose(2, 1, 0, 3))
    wo = np.asarray(inp["L0_attn_Wo"], f32)
    out["wo"] = np.ascontiguousarray(wo.reshape(KC, 128, 4, 512).transpose(2, 0, 1, 3))
    out["nw"] = np.ascontiguousarray(
        np.asarray(inp["L0_attn_norm_weight"], f32).reshape(KC, 128).T)
    return out


class Ctx:
    def __init__(self, role, eng, sems, dsems):
        self.role, self.eng, self.sems, self.dsems = role, eng, sems, dsems
        self.cnt = {e: 0 for e in ENGS}
        self.dcnt = {k: 0 for k in dsems}

    def _wait(self, w):
        if w is None:
            return
        if w[0] == "dma":
            self.eng.wait_ge(self.dsems[w[1]], w[2])
        elif w[1] > 0:
            self.eng.wait_ge(self.sems[w[0]], w[1])

    def op(self, role, fn, inc=False, waits=()):
        tok = None
        if inc:
            self.cnt[role] += 1
            tok = (role, self.cnt[role])
        if self.role == role:
            for w in waits:
                self._wait(w)
            instr = fn(self.eng)
            if inc:
                instr.then_inc(self.sems[role], 1)
        return tok

    def dma(self, role, dname, fn, waits=()):
        self.dcnt[dname] += 16
        tok = ("dma", dname, self.dcnt[dname])
        if self.role == role:
            for w in waits:
                self._wait(w)
            fn(self.eng).then_inc(self.dsems[dname], 16)
        return tok


class PPool:

    def __init__(self, slots):
        self.slots, self.n, self.cons = slots, 0, []

    def begin(self):
        i = self.n
        self.n += 1
        self.cons.append(None)
        guard = self.cons[i - len(self.slots)] if i >= len(self.slots) else None
        return self.slots[i % len(self.slots)], guard, i

    def done(self, i, tok):
        self.cons[i] = tok


LAYOUT = []


def build_program(debug=False, upto=None):
    LAYOUT.clear()
    nc = bass.Bass("TRN2", target_bir_lowering=False)

    x_h = nc.dram_tensor("x", [T, D], F32, kind="ExternalInput").ap()
    y_h = nc.dram_tensor("y", [T, D], F32, kind="ExternalOutput").ap()
    w_h = {}
    for nm, shp in (("wq", [128, KC, D]), ("wk", [128, KC, D]), ("wv", [128, KC, D]),
                    ("wg1", [128, KC, 128]), ("wg2", [128, D]), ("rw", [128, KC, 17]),
                    ("weg", [E, 4, 128, KC, 128]), ("weu", [E, 4, 128, KC, 128]),
                    ("wsg", [4, 128, KC, 128]), ("wsu", [4, 128, KC, 128]),
                    ("wed", [E, 4, 128, 4, 512]), ("wsd", [4, 128, 4, 512]),
                    ("wo", [4, KC, 128, 512]), ("nw", [128, KC]),
                    ("trig", [128, NCH, 128]), ("dcols", [128, 5 * H]),
                    ("mask", [128, 128]), ("identf", [128, 128]),
                    ("epsc", [128, 4]), ("ones_row", [1, 128])):
        w_h[nm] = nc.dram_tensor(nm, shp, F32, kind="ExternalInput").ap()
    w_h["identb"] = nc.dram_tensor("identb", [128, 128], BF16, kind="ExternalInput").ap()

    xnS = nc.dram_tensor("xnS", [NCH, 128, KC, 128], BF16, kind="Internal").ap()
    onS = nc.dram_tensor("onS", [NCH, 128, H, 128], BF16, kind="Internal").ap()

    lam = _lam()
    lamC = [float(np.exp(lam[h] * C)) for h in range(H)]

    dbg = {}
    if debug:
        for nm, shp, dt in (("xn", [128, KC, T], BF16), ("rlog", [128, NCH, 17], F32),
                            ("g1t", [128, T], BF16), ("qf", [128, G, T], BF16),
                            ("kf", [128, G, T], BF16), ("on", [NCH, 128, H, 128], BF16),
                            ("comb", [128, NCH, 18], F32), ("acc", [2, 128, 8, D], F32),
                            ("qtm", [128, 3, 512], BF16), ("rsc", [128, 2, 256], F32),
                            ("rdump", [128, 8, 24], F32),
                            ("lncols", [128, NCH, 8], F32), ("xst", [128, 2, 2048], F32),
                            ("vt", [128, VW, 512], BF16), ("ktm", [128, NCH, 512], BF16)):
            dbg[nm] = nc.dram_tensor("dbg_" + nm, shp, dt, kind="ExternalOutput").ap()

    dnames = ["d_cst", "d_cstg", "d_x0", "d_x1", "d_w0", "d_w1", "d_w2", "d_g2", "d_xs", "d_on",
              "d_xh", "d_fg0", "d_fg1", "d_fu0", "d_fu1", "d_xt2", "d_xt3",
              "d_yg", "d_on1",
              *[f"d_wo{i}" for i in range(2, KC)],
              "d_fd0", "d_fd1", "d_fd2", "d_fd3", "d_fd4",
              "d_wo0", "d_wo1", "d_onl0", "d_onl1", "d_xt0", "d_xt1", "d_y"]

    with contextlib.ExitStack() as st:
        ent = st.enter_context
        rlog_sb = ent(nc.sbuf_tensor("rlog_s", [128, NCH, 17], F32))
        epsc_sb = ent(nc.sbuf_tensor("epsc_s", [128, 4], F32))
        identf_sb = ent(nc.sbuf_tensor("identf_s", [128, 128], F32))
        identb_sb = ent(nc.sbuf_tensor("identb_s", [128, 128], BF16))
        cols_sb = ent(nc.sbuf_tensor("cols", [128, 32], F32))
        nw_sb = ent(nc.sbuf_tensor("nw_s", [128, KC], F32))
        ones_row_sb = ent(nc.sbuf_tensor("ones_row_s", [1, 128], F32))
        ARENA = 177 * 1024
        arena = ent(nc.sbuf_tensor("arena", [128, ARENA], mybir.dt.uint8))
        P = [ent(nc.psum_tensor(f"P{i}", [128, 512], F32)) for i in range(6)]
        Bb = [ent(nc.psum_tensor(f"B{i}", [128, 1024], BF16)) for i in range(2)]
        sems = {e: ent(nc.semaphore(f"sp_{e}")) for e in ENGS}
        dsems = {k: ent(nc.semaphore(k)) for k in dnames}
        block = ent(nc.Block())

        def av(off, shape, dtype, _nm=None):
            esz = 2 if dtype == BF16 else 4
            nb = int(np.prod(shape[1:])) * esz
            LAYOUT.append((off, tuple(shape), "bf16" if dtype == BF16 else "f32"))
            v = arena[:, off:off + nb].bitcast(dtype)
            if len(shape) > 2:
                nm = " ".join(f"a{i}" for i in range(1, len(shape)))
                v = v.rearrange(f"p ({nm}) -> p {nm}",
                                **{f"a{i}": int(shape[i]) for i in range(1, len(shape))})
            return v, off + ((nb + 31) // 32) * 32

        o = 0
        xn_fm, o = av(o, [128, KC, T], BF16)
        FR = o
        trig_sb, o = av(o, [128, NCH, 128], F32)
        dcols_sb, o = av(o, [128, 5 * H], F32)
        mask_sb, o = av(o, [128, 128], F32)
        S_sb, o = av(o, [128, G, 128], F32)
        Sbf_sb, o = av(o, [128, G, 128], BF16)
        g1t_sb, o = av(o, [128, T], BF16)
        A0B = o
        qfm, o = av(o, [128, G, T], BF16)
        kfm, o = av(o, [128, G, T], BF16)
        ktm, o = av(o, [128, NCH, 512], BF16)
        vtm, o = av(o, [128, VW, 512], BF16)
        wsl, o = av(o, [128, 3, KC, 256], BF16)
        wg2sl, o = av(o, [128, 1, 512], BF16)
        qtm, o = av(o, [128, 3, 512], BF16)
        rsc, o = av(o, [128, 2, 256], F32)
        sg_sb, o = av(o, [128, 512], F32)
        osc, o = av(o, [128, 2, 128], F32)
        scbf, o = av(o, [128, 2, 128], BF16)
        mscol, o = av(o, [128, 16], F32)
        rstdc, o = av(o, [128, 16], F32)
        ms2, o = av(o, [128, 16], F32)
        msn, o = av(o, [128, 16], F32)
        ontm, o = av(o, [128, 2, 128], BF16)
        onstg, o = av(o, [128, 1, 512], BF16)
        sqsc, o = av(o, [128, 128], BF16)
        dbg_lncols_sb, o = av(o, [128, NCH, 8], F32)
        onstg2, o = av(o, [128, 1, 512], BF16)
        assert o <= ARENA, f"phase A {o}"
        a0 = A0B
        xstage, a0 = av(a0, [128, 2, 2048], F32)
        xntm, a0 = av(a0, [128, 2, 2048], BF16)
        sqA0, a0 = av(a0, [128, 2048], F32)
        wg1_sb, a0 = av(a0, [128, KC, 128], BF16)
        rw_sb, a0 = av(a0, [128, KC, 17], BF16)
        o = 0
        xnh, o = av(o, [128, KC, 1024], BF16)
        acc, o = av(o, [128, 8, 2048], F32)
        SWB = o
        hbuf, o = av(o, [128, G, 4, 1024], BF16)
        fgsl, o = av(o, [128, 2, KC, 128], BF16)
        fusl, o = av(o, [128, 2, KC, 128], BF16)
        fdsl, o = av(o, [128, 4, 4, 512], BF16)
        bcst, o = av(o, [128, 2, 512], F32)
        hssc, o = av(o, [128, 2, 512], F32)
        comb_sb, o = av(o, [128, 8, 18], F32)
        rdump_sb, _ = av(SWB, [128, 8, 24], F32)
        wrk_sb, _ = av(SWB + 24 * 1024, [128, 8, 16], F32)
        crow_b = arena[0:1, o:o + 4096].bitcast(F32)
        crow = crow_b.rearrange("p (a b) -> p a b", a=1, b=1024)[:, 0]
        o += 4096
        fdsl4, o = av(o, [128, 1, 4, 512], BF16)
        assert o <= ARENA, f"phase F {o}"
        o = SWB
        wobf, o = av(o, [128, KC, 512], BF16)
        xts, o = av(o, [128, 4, 512], F32)
        assert o <= SWB + 24 * 1024, o - SWB
        o = SWB + 32 * 1024
        onls, o = av(o, [128, 2, KC, 128], BF16)
        outst, o = av(o, [128, 2, 512], F32)
        gsil, o = av(o, [128, 2, 512], F32)
        assert o <= SWB + 48 * 1024, o - SWB

        def program(ctx: Ctx):
            opn, dma = ctx.op, ctx.dma
            TEN, VEC, SCA, GPS, SYN = ENGS

            cst_sync = None
            for nm, dst in (("trig", trig_sb), ("dcols", dcols_sb), ("mask", mask_sb),
                            ("identf", identf_sb), ("identb", identb_sb),
                            ("epsc", epsc_sb), ("nw", nw_sb), ("ones_row", ones_row_sb)):
                cst_sync = dma(SYN, "d_cst", lambda e, nm=nm, dst=dst:
                               e.dma_start(dst[:], w_h[nm]))
            cst_gps = None
            for nm, dst in (("wg1", wg1_sb), ("rw", rw_sb)):
                cst_gps = dma(GPS, "d_cstg", lambda e, nm=nm, dst=dst:
                              e.dma_start(dst[:], w_h[nm]))
            cst_last = cst_sync

            poolP = PPool([P[0], P[1]])
            poolB = PPool([Bb[0], Bb[1]])

            mz = opn(VEC, lambda e: e.memset(cols_sb[:], 0.5), inc=True)
            opn(SCA, lambda e: e.preload_act_table(AF.Exp))
            w1 = opn(SCA, lambda e: e.activation(cols_sb[:, 15:16],
                                                 cols_sb[:, 14:15],
                                                 AF.Exp), inc=True, waits=[mz])
            opn(SCA, lambda e: e.preload_act_table(AF.Silu))
            w1 = opn(SCA, lambda e: e.activation(cols_sb[:, 15:16],
                                                 cols_sb[:, 14:15],
                                                 AF.Silu), inc=True, waits=[w1])
            opn(SCA, lambda e: e.preload_act_table(AF.Sqrt))
            w1 = opn(SCA, lambda e: e.activation(cols_sb[:, 15:16],
                                                 cols_sb[:, 14:15],
                                                 AF.Sqrt), inc=True, waits=[w1])
            wrm = opn(SCA, lambda e: e.activation(cols_sb[:, 15:16],
                                                  cols_sb[:, 14:15],
                                                  AF.Sqrt), inc=True, waits=[w1])
            opn(SCA, lambda e: e.nop(), waits=[cst_sync])

            spill_tok = [None] * NCH
            xn_evs = [None] * NCH
            xuse = [None] * NCH
            last_rl = last_g1 = None
            vv_par = {}
            xntm_guard = {}
            xt_tok = {}
            sq_tok = {}
            xt_tok[0] = dma(SYN, "d_x0", lambda e: e.dma_start(
                xstage[:, 0], x_h[0:128, :]), waits=[cst_sync])
            sq_tok[0] = opn(SCA, lambda e: e.activation(
                sqA0[:], xstage[:, 0], AF.Square, accum_out=cols_sb[:, 1:2]),
                inc=True, waits=[xt_tok[0], wrm])
            for c in range(NCH):
                sl = c % 2
                cb = sl * 11
                if c + 1 < NCH:
                    sl1 = (c + 1) % 2
                    xt_tok[c + 1] = dma(
                        SYN, f"d_x{sl1}", lambda e, c=c, sl1=sl1:
                        e.dma_start(xstage[:, sl1],
                                    x_h[(c + 1) * 128:(c + 2) * 128, :]),
                        waits=[xuse[c - 1]] if c + 1 >= 2 else [cst_sync])
                rs = opn(VEC, lambda e, sl=sl: e.reduce_sum(
                    cols_sb[:, cb + 0:cb + 1], xstage[:, sl], axis=AX), inc=True,
                    waits=[xt_tok[c]] + ([wrm] if c == 0 else []))
                if c + 1 < NCH:
                    cb1 = ((c + 1) % 2) * 11
                    sq_tok[c + 1] = opn(SCA, lambda e, sl1=(c + 1) % 2, cb1=cb1:
                                        e.activation(
                                            sqA0[:], xstage[:, sl1], AF.Square,
                                            accum_out=cols_sb[:, cb1 + 1:cb1 + 2]),
                                        inc=True,
                                        waits=[xt_tok[c + 1],
                                               vv_par.get((c + 1) % 2)])
                t_ = opn(VEC, lambda e: e.tensor_scalar_mul(cols_sb[:, cb + 2:cb + 3],
                                                            cols_sb[:, cb + 0:cb + 1],
                                                            1.0 / D), inc=True,
                         waits=[rs])
                t_ = opn(VEC, lambda e: e.tensor_scalar_mul(cols_sb[:, cb + 3:cb + 4],
                                                            cols_sb[:, cb + 1:cb + 2],
                                                            1.0 / D), inc=True,
                         waits=[sq_tok[c], t_])
                t_ = opn(VEC, lambda e: e.tensor_mul(cols_sb[:, cb + 4:cb + 5], cols_sb[:, cb + 2:cb + 3],
                                                     cols_sb[:, cb + 2:cb + 3]), inc=True,
                         waits=[t_])
                vv = opn(VEC, lambda e: e.tensor_sub(cols_sb[:, cb + 5:cb + 6], cols_sb[:, cb + 3:cb + 4],
                                                     cols_sb[:, cb + 4:cb + 5]), inc=True,
                         waits=[t_])
                vv_par[sl] = vv
                vv2 = opn(VEC, lambda e: e.tensor_scalar_add(cols_sb[:, cb + 9:cb + 10],
                                                             cols_sb[:, cb + 5:cb + 6],
                                                             epsc_sb[:, 0:1]),
                          inc=True, waits=[vv])
                lnt = opn(SCA, lambda e: e.activation(cols_sb[:, cb + 6:cb + 7], cols_sb[:, cb + 9:cb + 10],
                                                      AF.Sqrt), inc=True, waits=[vv2])
                rcp = opn(VEC, lambda e: e.reciprocal(cols_sb[:, cb + 7:cb + 8], cols_sb[:, cb + 6:cb + 7]),
                          inc=True, waits=[lnt])
                t_ = opn(VEC, lambda e: e.tensor_mul(cols_sb[:, cb + 8:cb + 9], cols_sb[:, cb + 7:cb + 8],
                                                     cols_sb[:, cb + 7:cb + 8]), inc=True,
                         waits=[rcp])
                t_ = opn(VEC, lambda e: e.tensor_mul(cols_sb[:, cb + 8:cb + 9], cols_sb[:, cb + 8:cb + 9],
                                                     cols_sb[:, cb + 9:cb + 10]), inc=True,
                         waits=[t_])
                t_ = opn(VEC, lambda e: e.tensor_scalar(cols_sb[:, cb + 8:cb + 9], cols_sb[:, cb + 8:cb + 9],
                                                        -0.5, 1.5, ALU.mult, ALU.add),
                         inc=True, waits=[t_])
                rstd = opn(VEC, lambda e: e.tensor_mul(cols_sb[:, cb + 7:cb + 8], cols_sb[:, cb + 7:cb + 8],
                                                       cols_sb[:, cb + 8:cb + 9]), inc=True,
                          waits=[t_])
                if debug:
                    opn(VEC, lambda e, c=c: e.tensor_copy(
                        dbg_lncols_sb[:, c, :], cols_sb[:, 0:8]), waits=[rstd])
                xnt = opn(VEC, lambda e, sl=sl: e.tensor_scalar(
                    xntm[:, sl], xstage[:, sl], cols_sb[:, cb + 2:cb + 3], cols_sb[:, cb + 7:cb + 8],
                    ALU.subtract, ALU.mult), inc=True,
                    waits=[rstd, xntm_guard.get(sl)])
                xuse[c] = xnt
                evs = []
                for half in range(2):
                    slot, guard, ui = poolB.begin()
                    tr = None
                    for k in range(8):
                        kc = half * 8 + k
                        tr = opn(TEN, lambda e, sl=sl, kc=kc, k=k, slot=slot:
                                 e.transpose(slot[:, k * 128:(k + 1) * 128],
                                             xntm[:, sl, kc * 128:(kc + 1) * 128],
                                             identb_sb[:]),
                                 inc=(k == 7), waits=([xnt, guard] if k == 0 else []))
                    ev = opn(SCA, lambda e, c=c, half=half, slot=slot: e.copy(
                        xn_fm[:, half * 8:(half + 1) * 8, c * 128:(c + 1) * 128],
                        slot[:].rearrange("p (k t) -> p k t", k=8)),
                        inc=True, waits=[tr])
                    poolB.done(ui, ev)
                    evs.append(ev)
                    xntm_guard[sl] = tr
                xn_evs[c] = evs[-1]
                spill_tok[c] = dma(SYN, "d_xs", lambda e, c=c:
                                   e.dma_start(xnS[c], xn_fm[:, :, c * 128:(c + 1) * 128]),
                                   waits=evs)
                slot, guard, ui = poolP.begin()
                mm = None
                for kc in range(KC):
                    mm = opn(TEN, lambda e, kc=kc, c=c, slot=slot: e.matmul(
                        slot[:, 0:17], xn_fm[:, kc, c * 128:(c + 1) * 128],
                        rw_sb[:, kc, :], start=(kc == 0), stop=(kc == KC - 1)),
                        inc=(kc == KC - 1),
                        waits=([guard, cst_gps] + evs if kc == 0 else []))
                last_rl = opn(SCA, lambda e, c=c, slot=slot: e.copy(
                    rlog_sb[:, c, :], slot[:, 0:17]), inc=True, waits=[mm])
                poolP.done(ui, last_rl)
                slot, guard, ui = poolP.begin()
                mm = None
                for kc in range(KC):
                    mm = opn(TEN, lambda e, kc=kc, c=c, slot=slot: e.matmul(
                        slot[:, 0:128], wg1_sb[:, kc, :],
                        xn_fm[:, kc, c * 128:(c + 1) * 128],
                        start=(kc == 0), stop=(kc == KC - 1)),
                        inc=(kc == KC - 1), waits=([guard] if kc == 0 else []))
                last_g1 = opn(SCA, lambda e, c=c, slot=slot: e.copy(
                    g1t_sb[:, c * 128:(c + 1) * 128], slot[:, 0:128]),
                    inc=True, waits=[mm])
                poolP.done(ui, last_g1)

            if debug:
                dma(SYN, "d_y", lambda e: e.dma_start(dbg["xn"], xn_fm[:]),
                    waits=[xn_evs[NCH - 1]])
                dma(SYN, "d_y", lambda e: e.dma_start(dbg["lncols"], dbg_lncols_sb[:]),
                    waits=[("vector", ctx.cnt["vector"])])
                dma(SYN, "d_y", lambda e: e.dma_start(dbg["xst"], xstage[:]),
                    waits=[("vector", ctx.cnt["vector"])])
                dma(SYN, "d_y", lambda e: e.dma_start(dbg["rlog"], rlog_sb[:]),
                    waits=[last_rl])
                dma(SYN, "d_y", lambda e: e.dma_start(dbg["g1t"], g1t_sb[:]),
                    waits=[last_g1])

            opn(SCA, lambda e: e.preload_act_table(AF.Silu))
            if upto == "A0":
                return

            slab_last = {}
            use_n = 0
            vtm_guard = {}
            onstg_guard = {}
            on_sp_last = None
            lastT = {0: None, 1: None}
            lastO = {0: None, 1: None}
            scbf_guard = {0: None, 1: None}
            ontm_guard = {0: None, 1: None}
            qtr_tok = {}
            rot_last = {}
            sbf_tok = {}
            rstdc_guard = [None]
            wsrc = ("wq", "wk", "wv")
            xg_last = None
            gate2_last = None

            for g in range(NG):
                sl_tok = {}

                def load_slab(si, ctx=ctx, g=g, sl_tok=None):
                    pass

                def emit_slab_load(si, g=g):
                    pr, half = si // 2, si % 2
                    un = use_n + si
                    gw = [slab_last.get(un - 3)]
                    assert un < 3 or (un - 3) in slab_last, f"slab guard missing {un}"
                    base = 512 * g + 256 * half
                    sl_tok[si] = dma(GPS, f"d_w{un % 3}",
                                     lambda e, pr=pr, base=base, ss=un % 3:
                                     e.dma_start(wsl[:, ss],
                                                 w_h[wsrc[pr]][:, :, base:base + 256]),
                                     waits=[w for w in gw if w])

                for si in range(3):
                    emit_slab_load(si)
                wg2_tok = dma(GPS, "d_g2", lambda e, g=g: e.dma_start(
                    wg2sl[:, 0], w_h["wg2"][:, 512 * g:512 * (g + 1)]),
                    waits=[gate2_last] if g > 0 else [])

                def pool_last(pool, nslots, slx):
                    idx = [i for i in range(len(pool.cons)) if i % nslots == slx]
                    return pool.cons[idx[-1]] if idx else None

                poolQ = PPool([P[0], P[1], P[2], P[3]])
                poolQ.cons = [pool_last(poolP, 2, 0), pool_last(poolP, 2, 1),
                              lastT[0], lastT[1]]
                poolQ.n = 4
                for pr in range(2):
                    rt_pend = {}

                    def emit_tr(tc, pr=pr, g=g):
                        slot, guard, ui = poolB.begin()
                        trs = None
                        src_base = qtm[:, tc % 3] if pr == 0 else ktm[:, tc]
                        for h in range(G):
                            trs = opn(TEN, lambda e, h=h, slot=slot, src_base=src_base:
                                      e.transpose(slot[:, h * 128:(h + 1) * 128],
                                                  src_base[:, h * 128:(h + 1) * 128],
                                                  identb_sb[:]),
                                      inc=(h == G - 1),
                                      waits=([rt_pend[tc], guard] if h == 0 else []))
                        if pr == 0:
                            qtr_tok[tc] = trs
                        dstf = qfm if pr == 0 else kfm
                        ev = opn(SCA, lambda e, tc=tc, slot=slot, dstf=dstf:
                                 e.copy(
                                     dstf[:, :, tc * 128:(tc + 1) * 128],
                                     slot[:, 0:512].rearrange("p (h t) -> p h t", h=4)),
                                 inc=True, waits=[trs])
                        poolB.done(ui, ev)

                    for tc in range(NCH):
                        rt = None
                        for half in range(2):
                            si = pr * 2 + half
                            un = use_n + si
                            slot, guard, ui = poolQ.begin()
                            ps = slot[:, 0:256]
                            mm = None
                            for kc in range(KC):
                                w0 = []
                                if kc == 0:
                                    w0 = [guard, sl_tok[si]]
                                    if g == 0 and pr == 0 and tc == 0:
                                        w0 += [xn_evs[NCH - 1], spill_tok[NCH - 1]]
                                mm = opn(TEN, lambda e, kc=kc, tc=tc, ss=un % 3, ps=ps:
                                         e.matmul(ps,
                                                  xn_fm[:, kc, tc * 128:(tc + 1) * 128],
                                                  wsl[:, ss, kc, :],
                                                  start=(kc == 0), stop=(kc == KC - 1)),
                                         inc=(kc == KC - 1), waits=w0)
                            if tc == NCH - 1:
                                slab_last[un] = mm
                            ev = opn(SCA, lambda e, ps=ps, half=half: e.activation(
                                rsc[:, half], ps, AF.Silu), inc=True,
                                waits=[mm, rot_last.get(half)])
                            poolQ.done(ui, ev)
                            dstp = qtm[:, tc % 3] if pr == 0 else ktm[:, tc]
                            for hh in range(2):
                                h = half * 2 + hh
                                a_ = rsc[:, half, hh * 128:hh * 128 + 64]
                                b_ = rsc[:, half, hh * 128 + 64:hh * 128 + 128]
                                cs = trig_sb[:, tc, 0:64]
                                sn = trig_sb[:, tc, 64:128]
                                dA = dstp[:, h * 128:h * 128 + 64]
                                dB = dstp[:, h * 128 + 64:h * 128 + 128]
                                t0, t1 = osc[:, 0, 0:64], osc[:, 1, 0:64]
                                ww = [ev] if hh == 0 else []
                                if pr == 0 and hh == 0 and half == 0:
                                    ww.append(qtr_tok.get(tc - 3))
                                opn(VEC, lambda e, a_=a_, cs=cs, t0=t0:
                                    e.tensor_mul(t0, a_, cs),
                                    waits=[w for w in ww if w])
                                e1 = opn(VEC, lambda e, b_=b_, sn=sn, t1=t1:
                                         e.tensor_mul(t1, b_, sn), inc=True)
                                opn(VEC, lambda e, dA=dA, t0=t0, t1=t1:
                                    e.tensor_sub(dA, t0, t1), waits=[e1])
                                opn(VEC, lambda e, a_=a_, sn=sn, t0=t0:
                                    e.tensor_mul(t0, a_, sn))
                                e2 = opn(VEC, lambda e, b_=b_, cs=cs, t1=t1:
                                         e.tensor_mul(t1, b_, cs), inc=True)
                                last = opn(VEC, lambda e, dB=dB, t0=t0, t1=t1:
                                           e.tensor_add(dB, t0, t1),
                                           inc=True, waits=[e2])
                            rot_last[half] = last
                            if half == 1:
                                rt = last
                        if pr == 1:
                            for h in range(G):
                                hg = 4 * g + h
                                rt = opn(VEC, lambda e, tc=tc, h=h, hg=hg:
                                         e.tensor_scalar_mul(
                                             ktm[:, tc, h * 128:(h + 1) * 128],
                                             ktm[:, tc, h * 128:(h + 1) * 128],
                                             dcols_sb[:, H + hg:H + hg + 1]),
                                         inc=(h == G - 1))
                        rt_pend[tc] = rt
                        if tc >= 2:
                            emit_tr(tc - 2)
                        if tc == NCH - 1:
                            emit_tr(NCH - 1 - 1)
                            emit_tr(NCH - 1)
                    if pr == 0:
                        emit_slab_load(3)
                        emit_slab_load(4)
                    else:
                        emit_slab_load(5)
                use_n += 4
                kdk_ready = ("vector", ctx.cnt["vector"])
                poolP = PPool([P[0], P[1]])
                poolP.cons = [pool_last(poolQ, 4, 0), pool_last(poolQ, 4, 1)]
                poolP.n = 2
                lastT[0] = pool_last(poolQ, 4, 2) or lastT[0]
                lastT[1] = pool_last(poolQ, 4, 3) or lastT[1]

                vev = {}
                sup_tok = {}
                msw_batch = []
                for tc in range(NCH):
                    for half in range(2):
                        si = 4 + half
                        un = use_n + half
                        slot, guard, ui = poolP.begin()
                        ps = slot[:, 0:256]
                        mm = None
                        for kc in range(KC):
                            w0 = [guard, sl_tok[si]] if kc == 0 else []
                            mm = opn(TEN, lambda e, kc=kc, tc=tc, ss=un % 3, ps=ps:
                                     e.matmul(ps, xn_fm[:, kc, tc * 128:(tc + 1) * 128],
                                              wsl[:, ss, kc, :],
                                              start=(kc == 0), stop=(kc == KC - 1)),
                                     inc=(kc == KC - 1), waits=[w for w in w0 if w])
                        if tc == NCH - 1:
                            slab_last[un] = mm
                        ww = [mm, vtm_guard.get(tc % VW)]
                        ev = opn(SCA, lambda e, tc=tc, half=half, ps=ps: e.copy(
                            vtm[:, tc % VW, half * 256:(half + 1) * 256], ps),
                            inc=True, waits=[w for w in ww if w])
                        poolP.done(ui, ev)
                        if half == 1:
                            vev[tc] = ev
                    slot, guard, ui = poolP.begin()
                    mm = opn(TEN, lambda e, tc=tc, slot=slot, g=g: e.matmul(
                        slot[:, 0:512], g1t_sb[:, tc * 128:(tc + 1) * 128],
                        wg2sl[:, 0], start=True, stop=True),
                        inc=True, waits=[guard, wg2_tok])
                    gate2_last = mm
                    tnh = opn(SCA, lambda e, slot=slot: e.activation(
                        sg_sb[:], slot[:, 0:512], AF.Tanh, scale=0.5),
                        inc=True, waits=[mm, xg_last])
                    poolP.done(ui, tnh)

                    c = tc
                    fr = {}

                    def front(h, c=c, g=g):
                        hg = 4 * g + h
                        par = h % 2
                        psT = P[2 + par]
                        qt = qfm[:, h, c * 128:(c + 1) * 128]
                        kt = kfm[:, h, c * 128:(c + 1) * 128]
                        w0 = [lastT[par]]
                        if h == 0:
                            w0 += [kdk_ready]
                        mmT = opn(TEN, lambda e, psT=psT, kt=kt, qt=qt: e.matmul(
                            psT[:, 0:128], kt, qt, start=True, stop=True),
                            inc=True, waits=[w for w in w0 if w])
                        scb = opn(VEC, lambda e, psT=psT, par=par:
                                  e.tensor_mul(
                                      scbf[:, par], psT[:, 0:128],
                                      mask_sb[:]), inc=True,
                                  waits=[mmT, scbf_guard[par]])
                        sc_ = (0.5 if hg == 0 else
                               dcols_sb[:, 4 * H + hg:4 * H + hg + 1])
                        sgt = opn(VEC, lambda e, h=h, sc_=sc_: e.tensor_scalar(
                            sg_sb[:, h * 128:(h + 1) * 128],
                            sg_sb[:, h * 128:(h + 1) * 128],
                            sc_, sc_, ALU.mult, ALU.add),
                            inc=True, waits=([tnh] if h == 0 else []))
                        fr[h] = (scb, sgt)

                    def back(h, c=c, g=g):
                        hg = 4 * g + h
                        par = h % 2
                        psT, psO = P[2 + par], P[4 + par]
                        qt = qfm[:, h, c * 128:(c + 1) * 128]
                        vt = vtm[:, c % VW, h * 128:(h + 1) * 128]
                        kdkt = ktm[:, c, h * 128:(h + 1) * 128]
                        scb, sgt = fr[h]
                        msu = opn(TEN, lambda e, psT=psT, kdkt=kdkt, vt=vt: e.matmul(
                            psT[:, 128:256], kdkt, vt, start=True, stop=True),
                            inc=True, waits=[scb, vev[c] if h == 0 else None])
                        wI = [lastO[par]]
                        acc2 = (hg > 0 and c > 0)
                        mi = opn(TEN, lambda e, psO=psO, par=par, vt=vt,
                                 st=not acc2: e.matmul(
                            psO[:, 0:128], scbf[:, par], vt, start=True, stop=st),
                            inc=True, waits=[w for w in wI if w])
                        scbf_guard[par] = mi
                        m2 = mi
                        if c > 0 and hg == 0:
                            m2 = opn(TEN, lambda e, psO=psO, qt=qt, h=h: e.matmul(
                                psO[:, 128:256], qt, Sbf_sb[:, h, :],
                                start=True, stop=True), inc=True,
                                waits=[sbf_tok[h]])
                        elif c > 0:
                            m2 = opn(TEN, lambda e, psO=psO, qt=qt, h=h: e.matmul(
                                psO[:, 0:128], qt, Sbf_sb[:, h, :],
                                start=False, stop=True), inc=True,
                                waits=[sbf_tok[h]])
                        if c == 0:
                            sup = opn(VEC, lambda e, psT=psT, h=h: e.tensor_copy(
                                S_sb[:, h, :], psT[:, 128:256]), inc=True,
                                waits=[msu, sbf_tok.get(h)])
                        else:
                            sup = opn(VEC, lambda e, psT=psT, h=h, hg=hg:
                                      e.scalar_tensor_tensor(
                                          S_sb[:, h, :], S_sb[:, h, :], lamC[hg],
                                          psT[:, 128:256], ALU.mult, ALU.add),
                                      inc=True, waits=[msu, sbf_tok.get(h)])
                        lastT[par] = sup
                        if hg == 0:
                            sbf_tok[h] = opn(SCA, lambda e, h=h: e.copy(
                                Sbf_sb[:, h, :], S_sb[:, h, :]), inc=True,
                                waits=[sup, m2])
                        else:
                            sbf_tok[h] = opn(SCA, lambda e, h=h, hg=hg: e.mul(
                                Sbf_sb[:, h, :], S_sb[:, h, :], lamC[hg]),
                                inc=True, waits=[sup, m2])
                        if hg == 0:
                            if c == 0:
                                mg = opn(VEC, lambda e, psO=psO, hg=hg:
                                         e.tensor_scalar_mul(
                                             osc[:, 0, :], psO[:, 0:128],
                                             dcols_sb[:, 3 * H + hg:3 * H + hg + 1]),
                                         inc=True, waits=[mi])
                            else:
                                mg = opn(VEC, lambda e, psO=psO, hg=hg:
                                         e.tensor_scalar_mul(
                                             osc[:, 1, :], psO[:, 0:128],
                                             dcols_sb[:, 3 * H + hg:3 * H + hg + 1]),
                                         inc=True, waits=[mi, m2])
                                mg = opn(VEC, lambda e, psO=psO, hg=hg:
                                         e.scalar_tensor_tensor(
                                             osc[:, 0, :], psO[:, 128:256],
                                             dcols_sb[:, 2 * H + hg:2 * H + hg + 1],
                                             osc[:, 1, :], ALU.mult, ALU.add),
                                         inc=True, waits=[mg])
                            xg = opn(VEC, lambda e, vt=vt, h=h: e.tensor_mul(
                                vt, osc[:, 0, :], sg_sb[:, h * 128:(h + 1) * 128]),
                                inc=True, waits=[sgt, mg])
                        else:
                            xg = opn(VEC, lambda e, vt=vt, h=h, psO=psO:
                                     e.tensor_mul(
                                         vt, psO[:, 0:128],
                                         sg_sb[:, h * 128:(h + 1) * 128]),
                                     inc=True, waits=[sgt, m2])
                        lastO[par] = xg
                        mst = opn(SCA, lambda e, vt=vt, c=c, h=h: e.activation(
                            sqsc[:], vt, AF.Square,
                            accum_out=mscol[:, (c % 4) * 4 + h:(c % 4) * 4 + h + 1]),
                            inc=True, waits=[xg])
                        msw_batch.append(mst)
                        return xg

                    for h in range(G):
                        front(h)
                        if h >= 1:
                            xg_last = back(h - 1)
                    xg_last = back(G - 1)
                    if c % 4 == 3:
                        msc = opn(VEC, lambda e: e.tensor_scalar(
                            ms2[:], mscol[:], 1.0 / HD, epsc_sb[:, 1:2],
                            ALU.mult, ALU.add), inc=True, waits=[msw_batch[-1]])
                        opn(SCA, lambda e: e.preload_act_table(AF.Sqrt))
                        lnt2 = opn(SCA, lambda e: e.activation(
                            rstdc[:], ms2[:], AF.Sqrt), inc=True,
                            waits=[msc, rstdc_guard[0]])
                        rcp2 = opn(VEC, lambda e: e.reciprocal(rstdc[:], rstdc[:]),
                                   inc=True, waits=[lnt2])
                        t2_ = opn(VEC, lambda e: e.tensor_mul(msn[:], rstdc[:],
                                                              rstdc[:]), inc=True,
                                  waits=[rcp2])
                        t2_ = opn(VEC, lambda e: e.tensor_mul(msn[:], msn[:], ms2[:]),
                                  inc=True, waits=[t2_])
                        t2_ = opn(VEC, lambda e: e.tensor_scalar(
                            msn[:], msn[:], -0.5, 1.5, ALU.mult, ALU.add),
                            inc=True, waits=[t2_])
                        rst = opn(VEC, lambda e: e.tensor_mul(rstdc[:], rstdc[:],
                                                              msn[:]), inc=True,
                                  waits=[t2_])
                        opn(SCA, lambda e: e.preload_act_table(AF.Silu))
                        msw_batch = []
                        ntoks = {}
                        for cc in range(c - 3, c + 1):
                            for h in range(G):
                                col = (cc % 4) * 4 + h
                                ww = [rst if (h == 0 and cc == c - 3) else None]
                                ntoks[(cc, h)] = opn(
                                    VEC, lambda e, cc=cc, h=h, col=col:
                                    e.tensor_scalar_mul(
                                        vtm[:, cc % VW, h * 128:(h + 1) * 128],
                                        vtm[:, cc % VW, h * 128:(h + 1) * 128],
                                        rstdc[:, col:col + 1]),
                                    inc=True, waits=[w for w in ww if w])
                        for cc in range(c - 3, c + 1):
                            slot, guard, ui = poolB.begin()
                            trs = None
                            for h in range(G):
                                trs = opn(TEN, lambda e, cc=cc, h=h, slot=slot:
                                          e.transpose(
                                              slot[:, h * 128:(h + 1) * 128],
                                              vtm[:, cc % VW,
                                                  h * 128:(h + 1) * 128],
                                              identb_sb[:]), inc=(h == G - 1),
                                          waits=([ntoks[(cc, G - 1)], guard]
                                                 if h == 0 else []))
                            vtm_guard[cc % VW] = trs
                            rstdc_guard[0] = ntoks[(c, G - 1)]
                            osl = cc % 2
                            obuf = onstg if osl == 0 else onstg2
                            ev = None
                            for h in range(G):
                                hg = 4 * g + h
                                ev = opn(SCA, lambda e, slot=slot, h=h, hg=hg,
                                         obuf=obuf:
                                         e.mul(obuf[:, 0, h * 128:(h + 1) * 128],
                                               slot[:, h * 128:(h + 1) * 128],
                                               nw_sb[:, hg:hg + 1]),
                                         inc=(h == G - 1),
                                         waits=([trs, onstg_guard.get(osl)]
                                                if h == 0 else []))
                            poolB.done(ui, ev)
                            spt = dma(SYN, "d_on" if osl == 0 else "d_on1",
                                      lambda e, cc=cc, g=g, obuf=obuf: e.dma_start(
                                          onS[cc, :, 4 * g:4 * (g + 1), :],
                                          obuf[:, 0].rearrange("p (h t) -> p h t",
                                                               h=4)),
                                      waits=[ev])
                            onstg_guard[osl] = spt
                            on_sp_last = spt
                use_n += 2

            if debug:
                jn = [(e2, ctx.cnt[e2]) for e2 in ("tensor", "vector", "scalar")]
                dma(SYN, "d_y", lambda e: e.dma_start(dbg["qf"], qfm[:]), waits=jn)
                dma(SYN, "d_y", lambda e: e.dma_start(dbg["kf"], kfm[:]))
                dma(SYN, "d_y", lambda e: e.dma_start(dbg["qtm"], qtm[:]))
                dma(SYN, "d_y", lambda e: e.dma_start(dbg["rsc"], rsc[:]))
                dma(SYN, "d_y", lambda e: e.dma_start(dbg["vt"], vtm[:]))
                dma(SYN, "d_y", lambda e: e.dma_start(dbg["ktm"], ktm[:]))
                dma(SYN, "d_y", lambda e: e.dma_start(dbg["on"], onS[:]),
                    waits=[on_sp_last])
            dump_gate = [("dma", "d_y", ctx.dcnt["d_y"])] if debug else []

            if upto == "A":
                return
            joinA = [(e2, ctx.cnt[e2]) for e2 in ("tensor", "vector", "scalar")]
            poolF = PPool([P[0], P[1], P[2], P[3]])
            poolD = PPool([P[4], P[5]])
            groups = [[0, 1, 2, 3], [4, 5, 6, 7], [8, 9, 10, 11], [12, 13, 14, 15], [16]]
            fg_guard = {}
            fu_guard = {}
            fd_ptr = [0]
            fd_slot_tok = {}

            def fd_ap(sl5):
                return fdsl[:, sl5] if sl5 < 4 else fdsl4[:, 0]
            hssc_guard = {}
            bcst_guard = {}
            crow_guard = None
            y_last = None
            sweep_done = None
            router_gate = None

            for half in range(2):
                hw = joinA + [on_sp_last, spill_tok[NCH - 1]] if half == 0 \
                    else [sweep_done]
                rtw = hw + dump_gate if half == 0 else router_gate
                xh = dma(SYN, "d_xh", lambda e, half=half: e.dma_start(
                    xnh[:], xnS[half * 8:(half + 1) * 8].transpose([1, 2, 0, 3])),
                    waits=hw)
                opn(SCA, lambda e: e.preload_act_table(AF.Exp))
                sme = opn(SCA, lambda e, half=half: e.activation(
                    comb_sb[:, :, 0:16], rlog_sb[:, half * 8:(half + 1) * 8, 0:16],
                    AF.Exp), inc=True, waits=rtw)
                th = opn(SCA, lambda e, half=half: e.activation(
                    comb_sb[:, :, 16:17], rlog_sb[:, half * 8:(half + 1) * 8, 16:17],
                    AF.Tanh, scale=0.5), inc=True, waits=[sme])
                shg_t = opn(VEC, lambda e: e.tensor_scalar(
                    comb_sb[:, :, 16:17], comb_sb[:, :, 16:17],
                    0.5, 0.5, ALU.mult, ALU.add), inc=True, waits=[th])
                tok_c = {}
                for tcl in range(8):
                    tok_c[tcl] = opn(VEC, lambda e, tcl=tcl: e.tensor_copy(
                        wrk_sb[:, tcl, :], comb_sb[:, tcl, 0:16]),
                        inc=True, waits=[sme] if tcl == 0 else [])
                for it in range(3):
                    mx_t = {}
                    for tcl in range(8):
                        mx_t[tcl] = opn(VEC, lambda e, tcl=tcl: e.reduce_max(
                            cols_sb[:, tcl:tcl + 1], wrk_sb[:, tcl, :], axis=AX),
                            inc=True, waits=[tok_c[tcl]])
                    for tcl in range(8):
                        tok_c[tcl] = opn(VEC, lambda e, tcl=tcl:
                                         e.scalar_tensor_tensor(
                                             wrk_sb[:, tcl, :], wrk_sb[:, tcl, :],
                                             cols_sb[:, tcl:tcl + 1],
                                             wrk_sb[:, tcl, :],
                                             ALU.is_lt, ALU.mult),
                                         inc=True, waits=[mx_t[tcl]])
                m4_t = {}
                for tcl in range(8):
                    m4_t[tcl] = opn(VEC, lambda e, tcl=tcl: e.reduce_max(
                        cols_sb[:, tcl:tcl + 1], wrk_sb[:, tcl, :], axis=AX),
                        inc=True, waits=[tok_c[tcl]])
                sg_t = {}
                for tcl in range(8):
                    sg_t[tcl] = opn(VEC, lambda e, tcl=tcl: e.scalar_tensor_tensor(
                        comb_sb[:, tcl, 0:16], comb_sb[:, tcl, 0:16],
                        cols_sb[:, tcl:tcl + 1], comb_sb[:, tcl, 0:16],
                        ALU.is_ge, ALU.mult), inc=True, waits=[m4_t[tcl]])
                ss_t = {}
                for tcl in range(8):
                    ss_t[tcl] = opn(VEC, lambda e, tcl=tcl: e.reduce_sum(
                        cols_sb[:, 8 + tcl:9 + tcl], comb_sb[:, tcl, 0:16], axis=AX),
                        inc=True, waits=[sg_t[tcl]])
                rc_t = {}
                for tcl in range(8):
                    rc_t[tcl] = opn(VEC, lambda e, tcl=tcl: e.reciprocal(
                        cols_sb[:, 16 + tcl:17 + tcl], cols_sb[:, 8 + tcl:9 + tcl]),
                        inc=True, waits=[ss_t[tcl]])
                comb_done = {}
                for tcl in range(8):
                    comb_done[tcl] = opn(VEC, lambda e, tcl=tcl: e.tensor_scalar_mul(
                        comb_sb[:, tcl, 0:16], comb_sb[:, tcl, 0:16],
                        cols_sb[:, 16 + tcl:17 + tcl]), inc=True,
                        waits=[rc_t[tcl]] + ([shg_t] if tcl == 0 else []))
                if debug:
                    dma(SYN, "d_y", lambda e, half=half: e.dma_start(
                        dbg["comb"][:, half * 8:(half + 1) * 8, :],
                        comb_sb[:]), waits=[comb_done[7]])
                opn(SCA, lambda e: e.preload_act_table(AF.Silu))

                acc_tok = {}
                hb_guard = {}
                sw_state = {"wo_guard": {}, "onl_guard": {}, "xt_guard": {},
                            "out_guard": {}, "gs_guard": {}, "wcv": {},
                            "last_t2": None, "mms_last": None}

                def emit_wo_stream(jq, half=half):
                    gw = [sw_state["mms_last"]]
                    if jq == 0:
                        gw += [hb_guard.get((0, 0, 0)), hb_guard.get((1, 0, 0))]
                    toks = []
                    for kc in range(KC):
                        toks.append(dma(GPS, f"d_wo{kc}", lambda e, jq=jq, kc=kc:
                                    e.dma_start(wobf[:, kc, :], w_h["wo"][jq, kc]),
                                    waits=[w for w in gw if w] if kc == 0 else []))
                    sw_state["wcv"][jq] = toks

                def emit_sweep(jq, half=half):
                    nonlocal y_last
                    for tcl in range(8):
                        cg = half * 8 + tcl
                        ts_ = tcl % 2
                        gw = [sw_state["onl_guard"].get(ts_)]
                        if jq == 0 and tcl < 2:
                            gw.append(fg_guard.get(tcl % 2))
                            gw.append(fu_guard.get(tcl % 2))
                            gw.append(("dma", "d_on", ctx.dcnt["d_on"]))
                            gw.append(("dma", "d_on1", ctx.dcnt["d_on1"]))
                        ot = dma(SYN, f"d_onl{ts_}", lambda e, cg=cg, ts_=ts_:
                                 e.dma_start(onls[:, ts_],
                                             onS[cg]),
                                 waits=[w for w in gw if w])
                        ts4 = tcl % 4
                        gw = [sw_state["xt_guard"].get(ts4)]
                        if jq == 0 and tcl < 4:
                            gw.append(hb_guard.get((2, 0, 0)))
                        xt = dma(SYN, f"d_xt{ts4}", lambda e, cg=cg, jq=jq, ts4=ts4:
                                 e.dma_start(xts[:, ts4],
                                             x_h[cg * 128:(cg + 1) * 128,
                                                 jq * 512:(jq + 1) * 512]),
                                 waits=[w for w in gw if w])
                        slot, guard, ui = poolF.begin()
                        mms = None
                        for kc in range(KC):
                            w0 = [guard, ot] if kc == 0 else []
                            if tcl == 0:
                                w0 = w0 + [sw_state["wcv"][jq][kc]]
                            mms = opn(TEN, lambda e, kc=kc, ts_=ts_, slot=slot:
                                      e.matmul(slot[:, 0:512], onls[:, ts_, kc, :],
                                               wobf[:, kc, :],
                                               start=(kc == 0), stop=(kc == KC - 1)),
                                      inc=(kc == KC - 1), waits=[w for w in w0 if w])
                        sw_state["onl_guard"][ts_] = mms
                        sw_state["mms_last"] = mms
                        gs = opn(SCA, lambda e, tcl=tcl, jq=jq, ts_=ts_: e.activation(
                            gsil[:, ts_], acc[:, tcl, jq * 512:(jq + 1) * 512],
                            AF.Silu), inc=True,
                            waits=[acc_tok[(tcl, jq)], sw_state["gs_guard"].get(ts_)])
                        t1 = opn(VEC, lambda e, ts_=ts_, slot=slot: e.tensor_mul(
                            outst[:, ts_], slot[:, 0:512], gsil[:, ts_]),
                            inc=True, waits=[mms, gs, sw_state["out_guard"].get(ts_)])
                        poolF.done(ui, t1)
                        sw_state["gs_guard"][ts_] = t1
                        t2 = opn(VEC, lambda e, ts_=ts_, ts4=ts4: e.tensor_add(
                            outst[:, ts_], outst[:, ts_], xts[:, ts4]),
                            inc=True, waits=[xt])
                        sw_state["xt_guard"][ts4] = t2
                        yw = dma(GPS, "d_yg", lambda e, cg=cg, jq=jq, ts_=ts_:
                                 e.dma_start(y_h[cg * 128:(cg + 1) * 128,
                                                 jq * 512:(jq + 1) * 512],
                                             outst[:, ts_]), waits=[t2])
                        sw_state["out_guard"][ts_] = yw
                        sw_state["last_t2"] = t2
                        y_last = yw

                for gi, grp in enumerate(groups):
                    hready = {}
                    fdt_pre = {}
                    for ei, ex in enumerate(grp):
                        evc = None
                        for hr in range(2):
                            slot, guard, ui = poolF.begin()
                            trc = None
                            for tt in range(4):
                                tcl = hr * 4 + tt
                                col = 16 if ex == 16 else ex
                                trc = opn(TEN, lambda e, tt=tt, tcl=tcl, col=col,
                                          slot=slot:
                                          e.transpose(slot[0:1, tt * 128:tt * 128 + 128],
                                                      comb_sb[:, tcl, col:col + 1],
                                                      identf_sb[:]),
                                          inc=(tt == 3),
                                          waits=([guard, comb_done[7], crow_guard]
                                                 if tt == 0 else []))
                            evc = opn(VEC, lambda e, hr=hr, slot=slot: e.tensor_copy(
                                crow[0:1, hr * 512:(hr + 1) * 512], slot[0:1, 0:512]),
                                inc=True, waits=[trc])
                            poolF.done(ui, evc)
                        bc_tok = [None, None]
                        for nt in range(2):
                            slot, guard, ui = poolF.begin()
                            mmb = opn(TEN, lambda e, nt=nt, slot=slot: e.matmul(
                                slot[:, 0:512], ones_row_sb[:],
                                crow[0:1, nt * 512:(nt + 1) * 512],
                                start=True, stop=True), inc=True, waits=[guard, evc])
                            if nt == 1:
                                crow_guard = mmb
                            bev = opn(SCA, lambda e, nt=nt, slot=slot: e.copy(
                                bcst[:, nt], slot[:, 0:512]), inc=True, waits=[mmb])
                            poolF.done(ui, bev)
                            bc_tok[nt] = bev
                        hlast = None
                        for mf in range(4):
                            gsl = (ex * 4 + mf) % 2 if ex != 16 else mf % 2
                            gidx = 4 * (ex if ex != 16 else 0) + mf
                            wsrc_g = (w_h["weg"][ex, mf] if ex != 16
                                      else w_h["wsg"][mf])
                            wsrc_u = (w_h["weu"][ex, mf] if ex != 16
                                      else w_h["wsu"][mf])
                            fgt = dma(GPS, f"d_fg{gsl}", lambda e, w=wsrc_g, gsl=gsl:
                                      e.dma_start(fgsl[:, gsl], w),
                                      waits=[fg_guard.get(gsl),
                                             xh if (gi == 0 and mf == 0) else None])
                            fut = dma(GPS, f"d_fu{gsl}", lambda e, w=wsrc_u, gsl=gsl:
                                      e.dma_start(fusl[:, gsl], w),
                                      waits=[fu_guard.get(gsl)])
                            for nt in range(2):
                                slotg, guardg, uig = poolF.begin()
                                mmg = None
                                for kc in range(KC):
                                    w0 = ([guardg, fgt, xh] if kc == 0 else [])
                                    if kc == 0:
                                        w0.append(hb_guard.get((3 if gi == 4 else ei,
                                                                mf, nt)))
                                    mmg = opn(TEN, lambda e, kc=kc, nt=nt, gsl=gsl,
                                              slotg=slotg: e.matmul(
                                                  slotg[:, 0:512],
                                                  fgsl[:, gsl, kc, :],
                                                  xnh[:, kc, nt * 512:(nt + 1) * 512],
                                                  start=(kc == 0), stop=(kc == KC - 1)),
                                              inc=(kc == KC - 1),
                                              waits=[w for w in w0 if w])
                                slotu, guardu, uiu = poolF.begin()
                                mmu = None
                                for kc in range(KC):
                                    w0 = [guardu, fut] if kc == 0 else []
                                    mmu = opn(TEN, lambda e, kc=kc, nt=nt, gsl=gsl,
                                              slotu=slotu: e.matmul(
                                                  slotu[:, 0:512],
                                                  fusl[:, gsl, kc, :],
                                                  xnh[:, kc, nt * 512:(nt + 1) * 512],
                                                  start=(kc == 0), stop=(kc == KC - 1)),
                                              inc=(kc == KC - 1),
                                              waits=[w for w in w0 if w])
                                if mf == 3 and nt == 1:
                                    fg_guard[gsl] = mmg
                                    fu_guard[gsl] = mmu
                                else:
                                    fg_guard[gsl] = mmg
                                    fu_guard[gsl] = mmu
                                hs_ = opn(SCA, lambda e, nt=nt, slotg=slotg:
                                          e.activation(hssc[:, nt], slotg[:, 0:512],
                                                       AF.Silu), inc=True,
                                          waits=[mmg, hssc_guard.get(nt)])
                                poolF.done(uig, hs_)
                                h1 = opn(VEC, lambda e, nt=nt, slotu=slotu:
                                         e.tensor_mul(hssc[:, nt], hssc[:, nt],
                                                      slotu[:, 0:512]),
                                         inc=True, waits=[hs_, mmu])
                                poolF.done(uiu, h1)
                                hsl = 3 if gi == 4 else ei
                                hlast = opn(VEC, lambda e, hsl=hsl, mf=mf, nt=nt:
                                            e.tensor_mul(
                                                hbuf[:, hsl, mf,
                                                     nt * 512:(nt + 1) * 512],
                                                hssc[:, nt], bcst[:, nt]),
                                            inc=True, waits=[bc_tok[nt]])
                                hssc_guard[nt] = hlast
                                bcst_guard[nt] = hlast
                        hready[ei] = hlast
                    fd_pre_slot = {}
                    for ei, ex in enumerate(grp):
                        sl5 = fd_ptr[0] % 5
                        fd_ptr[0] += 1
                        fd_pre_slot[ei] = sl5
                        wsrc_d0 = (w_h["wed"][ex, 0] if ex != 16 else w_h["wsd"][0])
                        fdt_pre[ei] = dma(GPS, f"d_fd{sl5}",
                                          lambda e, w=wsrc_d0, sl5=sl5:
                                          e.dma_start(fd_ap(sl5), w),
                                          waits=[fd_slot_tok.get(sl5)])
                    if gi == 4:
                        emit_wo_stream(0)
                    fdt_nxt = {}
                    slot_nxt = {}
                    for dq in range(4):
                        if dq == 0:
                            fdt = fdt_pre
                            dq_slot = fd_pre_slot
                        else:
                            fdt = fdt_nxt
                            dq_slot = slot_nxt
                        for tcl in range(8):
                            slot, guard, ui = poolD.begin()
                            mmd = None
                            for ei, ex in enumerate(grp):
                                hsl = 3 if gi == 4 else ei
                                for kf in range(4):
                                    w0 = []
                                    if kf == 0:
                                        w0 = [hready[ei], fdt[ei]]
                                        if ei == 0:
                                            w0.append(guard)
                                    mmd = opn(TEN, lambda e, hsl=hsl, kf=kf, tcl=tcl,
                                              dq=dq, slot=slot, ei=ei,
                                              sl5=dq_slot[ei],
                                              ng=len(grp): e.matmul(
                                                  slot[:, 0:512],
                                                  hbuf[:, hsl, kf,
                                                       tcl * 128:(tcl + 1) * 128],
                                                  fd_ap(sl5)[:, kf, :],
                                                  start=(ei == 0 and kf == 0),
                                                  stop=(ei == ng - 1 and kf == 3)),
                                              inc=(ei == len(grp) - 1 and kf == 3),
                                              waits=[w for w in w0 if w])
                            if tcl == 7:
                                for ei in range(len(grp)):
                                    fd_slot_tok[dq_slot[ei]] = mmd
                            at = None
                            tgt = acc[:, tcl, dq * 512:(dq + 1) * 512]
                            if gi == 0:
                                at = opn(VEC, lambda e, tgt=tgt, slot=slot:
                                         e.tensor_copy(tgt, slot[:, 0:512]),
                                         inc=True,
                                         waits=[mmd, acc_tok.get((tcl, dq))])
                            else:
                                at = opn(VEC, lambda e, tgt=tgt, slot=slot:
                                         e.tensor_add(tgt, tgt, slot[:, 0:512]),
                                         inc=True, waits=[mmd])
                            poolD.done(ui, at)
                            acc_tok[(tcl, dq)] = at
                        if dq < 3:
                            fdt_nxt = {}
                            slot_nxt = {}
                            for ei, ex in enumerate(grp):
                                sl5 = fd_ptr[0] % 5
                                fd_ptr[0] += 1
                                slot_nxt[ei] = sl5
                                wsrc_d = (w_h["wed"][ex, dq + 1] if ex != 16
                                          else w_h["wsd"][dq + 1])
                                fdt_nxt[ei] = dma(GPS, f"d_fd{sl5}",
                                                  lambda e, w=wsrc_d, sl5=sl5:
                                                  e.dma_start(fd_ap(sl5), w),
                                                  waits=[fd_slot_tok.get(sl5)])
                        if gi == 4:
                            emit_sweep(dq)
                            if dq < 3:
                                emit_wo_stream(dq + 1)
                    dtok = acc_tok[(7, 3)]
                    for ei in range(4):
                        for mf in range(4):
                            for nt in range(2):
                                hb_guard[(ei, mf, nt)] = dtok
                if debug:
                    dma(SYN, "d_y", lambda e, half=half: e.dma_start(
                        dbg["acc"][half], acc[:]), waits=[acc_tok[(7, 3)]])

                sweep_done = sw_state["last_t2"]
                router_gate = [acc_tok[(7, 3)], evc, bc_tok[1]]

            opn(SYN, lambda e: e.nop(), waits=[y_last])

        for role in ENGS:
            def mk(r):
                def _f(eng):
                    program(Ctx(r, eng, sems, dsems))
                return _f
            getattr(block, role)(mk(role))

    return nc, dbg


_NC_CACHE = {}


def kernel(**inputs):
    key = "main"
    if key not in _NC_CACHE:
        _NC_CACHE[key] = build_program(debug=False)
    nc, _ = _NC_CACHE[key]
    wl = host_relayout(inputs)
    cst = host_constants()
    x = np.asarray(inputs["L0_attn_x"], np.float32)
    base = dict(wl)
    base.update(cst)
    in_maps = []
    for core in range(B):
        m = dict(base)
        m["x"] = np.ascontiguousarray(x[core])
        in_maps.append(m)
    res = run_bass_kernel_spmd(nc, in_maps, core_ids=list(range(B)))
    out = np.empty((B, T, D), np.float32)
    for core in range(B):
        out[core] = res.results[core]["y"]
    return out


if __name__ == "__main__":
    nc, _ = build_program(debug=True)
    print("built", nc.n_instructions())


# version 18
# speedup vs baseline: 1.0167x; 1.0167x over previous
import contextlib
import numpy as np
import ml_dtypes
import concourse.bass as bass
import concourse.mybir as mybir
from concourse.bass_utils import run_bass_kernel_spmd

F32 = mybir.dt.float32
BF16 = mybir.dt.bfloat16
AF = mybir.ActivationFunctionType
ALU = mybir.AluOpType
AX = mybir.AxisListType.X

B, T, D = 8, 2048, 2048
H, HD, C = 16, 128, 128
NCH, KC = T // C, D // 128
E, FF = 16, 512
G, NG = 4, 4
LN_EPS, RMS_EPS = 1e-5, 1e-6
ROPE_BASE = 10000.0
CTR = (C - 1) / 2.0
VW = 8

ENGS = ("tensor", "vector", "scalar", "gpsimd", "sync")


def _lam():
    h = np.arange(H, dtype=np.float64)
    return -(2.0 ** (-8.0 * (h + 1.0) / H))


def host_constants():
    lam = _lam()
    tcv = np.arange(C, dtype=np.float64)
    cst = {}
    j = np.arange(HD // 2, dtype=np.float64)
    theta = ROPE_BASE ** (-2.0 * j / HD)
    pos = np.arange(T, dtype=np.float64).reshape(NCH, C)
    ang = pos[:, :, None] * theta[None, None, :]
    cst["trig"] = np.concatenate(
        [np.cos(ang).transpose(1, 0, 2), np.sin(ang).transpose(1, 0, 2)], axis=2
    ).astype(np.float32)
    dk = np.exp(lam[None, :] * (C - 1.0 - tcv[:, None]))
    dk2 = np.exp(lam[None, :] * (CTR - tcv[:, None]))
    dqb = np.exp(lam[None, :] * (tcv[:, None] + 1.0 + CTR))
    dq2b = np.exp(lam[None, :] * (tcv[:, None] - CTR))
    cst["dcols"] = np.concatenate(
        [dk, dk2, dqb, dq2b, 0.5 * dq2b], axis=1).astype(np.float32)
    m1 = np.triu(np.ones((C, C), np.float32))
    cst["mask"] = m1
    cst["identb"] = np.eye(128, dtype=ml_dtypes.bfloat16)
    cst["identf"] = np.eye(128, dtype=np.float32)
    epsc = np.zeros((128, 4), np.float32)
    epsc[:, 0] = LN_EPS
    epsc[:, 1] = RMS_EPS
    cst["epsc"] = epsc
    cst["ones_row"] = np.ones((1, 128), np.float32)
    return cst


def host_relayout(inp):
    f32 = np.float32
    out = {}
    for nm, w in (("wq", inp["L0_attn_Wq"]), ("wk", inp["L0_attn_Wk"]),
                  ("wv", inp["L0_attn_Wv"])):
        out[nm] = np.ascontiguousarray(
            np.asarray(w, f32).reshape(KC, 128, D).transpose(1, 0, 2))
    out["wg1"] = np.ascontiguousarray(
        np.asarray(inp["L0_attn_Wg1"], f32).reshape(KC, 128, 128).transpose(1, 0, 2))
    out["wg2"] = np.ascontiguousarray(np.asarray(inp["L0_attn_Wg2"], f32))
    rw = np.concatenate([np.asarray(inp["L0_ffn_gate_w"], f32).T,
                         np.asarray(inp["L0_ffn_shared_gate_w"], f32)], axis=1)
    out["rw"] = np.ascontiguousarray(rw.reshape(KC, 128, 17).transpose(1, 0, 2))
    for nm, w in (("weg", inp["L0_ffn_Wg"]), ("weu", inp["L0_ffn_Wu"])):
        w = np.asarray(w, f32)
        out[nm] = np.ascontiguousarray(
            w.reshape(E, KC, 128, 4, 128).transpose(0, 3, 2, 1, 4))
    for nm, w in (("wsg", inp["L0_ffn_sWg"]), ("wsu", inp["L0_ffn_sWu"])):
        w = np.asarray(w, f32)
        out[nm] = np.ascontiguousarray(w.reshape(KC, 128, 4, 128).transpose(2, 1, 0, 3))
    wd = np.asarray(inp["L0_ffn_Wd"], f32)
    out["wed"] = np.ascontiguousarray(
        wd.reshape(E, 4, 128, 4, 512).transpose(0, 3, 2, 1, 4))
    wsd = np.asarray(inp["L0_ffn_sWd"], f32)
    out["wsd"] = np.ascontiguousarray(wsd.reshape(4, 128, 4, 512).transpose(2, 1, 0, 3))
    wo = np.asarray(inp["L0_attn_Wo"], f32)
    out["wo"] = np.ascontiguousarray(wo.reshape(KC, 128, 4, 512).transpose(2, 0, 1, 3))
    out["nw"] = np.ascontiguousarray(
        np.asarray(inp["L0_attn_norm_weight"], f32).reshape(KC, 128).T)
    return out


class Ctx:
    def __init__(self, role, eng, sems, dsems):
        self.role, self.eng, self.sems, self.dsems = role, eng, sems, dsems
        self.cnt = {e: 0 for e in ENGS}
        self.dcnt = {k: 0 for k in dsems}

    def _wait(self, w):
        if w is None:
            return
        if w[0] == "dma":
            self.eng.wait_ge(self.dsems[w[1]], w[2])
        elif w[1] > 0:
            self.eng.wait_ge(self.sems[w[0]], w[1])

    def op(self, role, fn, inc=False, waits=()):
        tok = None
        if inc:
            self.cnt[role] += 1
            tok = (role, self.cnt[role])
        if self.role == role:
            for w in waits:
                self._wait(w)
            instr = fn(self.eng)
            if inc:
                instr.then_inc(self.sems[role], 1)
        return tok

    def dma(self, role, dname, fn, waits=()):
        self.dcnt[dname] += 16
        tok = ("dma", dname, self.dcnt[dname])
        if self.role == role:
            for w in waits:
                self._wait(w)
            fn(self.eng).then_inc(self.dsems[dname], 16)
        return tok


class PPool:

    def __init__(self, slots):
        self.slots, self.n, self.cons = slots, 0, []

    def begin(self):
        i = self.n
        self.n += 1
        self.cons.append(None)
        guard = self.cons[i - len(self.slots)] if i >= len(self.slots) else None
        return self.slots[i % len(self.slots)], guard, i

    def done(self, i, tok):
        self.cons[i] = tok


LAYOUT = []


def build_program(debug=False, upto=None):
    LAYOUT.clear()
    nc = bass.Bass("TRN2", target_bir_lowering=False)

    x_h = nc.dram_tensor("x", [T, D], F32, kind="ExternalInput").ap()
    y_h = nc.dram_tensor("y", [T, D], F32, kind="ExternalOutput").ap()
    w_h = {}
    for nm, shp in (("wq", [128, KC, D]), ("wk", [128, KC, D]), ("wv", [128, KC, D]),
                    ("wg1", [128, KC, 128]), ("wg2", [128, D]), ("rw", [128, KC, 17]),
                    ("weg", [E, 4, 128, KC, 128]), ("weu", [E, 4, 128, KC, 128]),
                    ("wsg", [4, 128, KC, 128]), ("wsu", [4, 128, KC, 128]),
                    ("wed", [E, 4, 128, 4, 512]), ("wsd", [4, 128, 4, 512]),
                    ("wo", [4, KC, 128, 512]), ("nw", [128, KC]),
                    ("trig", [128, NCH, 128]), ("dcols", [128, 5 * H]),
                    ("mask", [128, 128]), ("identf", [128, 128]),
                    ("epsc", [128, 4]), ("ones_row", [1, 128])):
        w_h[nm] = nc.dram_tensor(nm, shp, F32, kind="ExternalInput").ap()
    w_h["identb"] = nc.dram_tensor("identb", [128, 128], BF16, kind="ExternalInput").ap()

    xnS = nc.dram_tensor("xnS", [NCH, 128, KC, 128], BF16, kind="Internal").ap()
    onS = nc.dram_tensor("onS", [NCH, 128, H, 128], BF16, kind="Internal").ap()

    lam = _lam()
    lamC = [float(np.exp(lam[h] * C)) for h in range(H)]

    dbg = {}
    if debug:
        for nm, shp, dt in (("xn", [128, KC, T], BF16), ("rlog", [128, NCH, 17], F32),
                            ("g1t", [128, T], BF16), ("qf", [128, G, T], BF16),
                            ("kf", [128, G, T], BF16), ("on", [NCH, 128, H, 128], BF16),
                            ("comb", [128, NCH, 18], F32), ("acc", [2, 128, 8, D], F32),
                            ("qtm", [128, 3, 512], BF16), ("rsc", [128, 2, 256], F32),
                            ("rdump", [128, 8, 24], F32),
                            ("lncols", [128, NCH, 8], F32), ("xst", [128, 2, 2048], F32),
                            ("vt", [128, VW, 512], BF16), ("ktm", [128, NCH, 512], BF16)):
            dbg[nm] = nc.dram_tensor("dbg_" + nm, shp, dt, kind="ExternalOutput").ap()

    dnames = ["d_cst", "d_cstg", "d_x0", "d_x1", "d_w0", "d_w1", "d_w2", "d_g2", "d_xs", "d_on",
              "d_xh", "d_fg0", "d_fg1", "d_fu0", "d_fu1", "d_xt2", "d_xt3",
              "d_yg", "d_on1",
              *[f"d_wo{i}" for i in range(2, KC)],
              "d_fd0", "d_fd1", "d_fd2", "d_fd3", "d_fd4",
              "d_wo0", "d_wo1", "d_onl0", "d_onl1", "d_xt0", "d_xt1", "d_y"]

    with contextlib.ExitStack() as st:
        ent = st.enter_context
        rlog_sb = ent(nc.sbuf_tensor("rlog_s", [128, NCH, 17], F32))
        epsc_sb = ent(nc.sbuf_tensor("epsc_s", [128, 4], F32))
        identf_sb = ent(nc.sbuf_tensor("identf_s", [128, 128], F32))
        identb_sb = ent(nc.sbuf_tensor("identb_s", [128, 128], BF16))
        cols_sb = ent(nc.sbuf_tensor("cols", [128, 32], F32))
        nw_sb = ent(nc.sbuf_tensor("nw_s", [128, KC], F32))
        ones_row_sb = ent(nc.sbuf_tensor("ones_row_s", [1, 128], F32))
        ARENA = 177 * 1024
        arena = ent(nc.sbuf_tensor("arena", [128, ARENA], mybir.dt.uint8))
        P = [ent(nc.psum_tensor(f"P{i}", [128, 512], F32)) for i in range(6)]
        Bb = [ent(nc.psum_tensor(f"B{i}", [128, 1024], BF16)) for i in range(2)]
        sems = {e: ent(nc.semaphore(f"sp_{e}")) for e in ENGS}
        dsems = {k: ent(nc.semaphore(k)) for k in dnames}
        block = ent(nc.Block())

        def av(off, shape, dtype, _nm=None):
            esz = 2 if dtype == BF16 else 4
            nb = int(np.prod(shape[1:])) * esz
            LAYOUT.append((off, tuple(shape), "bf16" if dtype == BF16 else "f32"))
            v = arena[:, off:off + nb].bitcast(dtype)
            if len(shape) > 2:
                nm = " ".join(f"a{i}" for i in range(1, len(shape)))
                v = v.rearrange(f"p ({nm}) -> p {nm}",
                                **{f"a{i}": int(shape[i]) for i in range(1, len(shape))})
            return v, off + ((nb + 31) // 32) * 32

        o = 0
        xn_fm, o = av(o, [128, KC, T], BF16)
        FR = o
        trig_sb, o = av(o, [128, NCH, 128], F32)
        dcols_sb, o = av(o, [128, 5 * H], F32)
        mask_sb, o = av(o, [128, 128], F32)
        S_sb, o = av(o, [128, G, 128], F32)
        Sbf_sb, o = av(o, [128, G, 128], BF16)
        g1t_sb, o = av(o, [128, T], BF16)
        A0B = o
        qfm, o = av(o, [128, G, T], BF16)
        kfm, o = av(o, [128, G, T], BF16)
        ktm, o = av(o, [128, NCH, 512], BF16)
        vtm, o = av(o, [128, VW, 512], BF16)
        wsl, o = av(o, [128, 3, KC, 256], BF16)
        wg2sl, o = av(o, [128, 1, 512], BF16)
        qtm, o = av(o, [128, 3, 512], BF16)
        rsc, o = av(o, [128, 2, 256], F32)
        sg_sb, o = av(o, [128, 512], F32)
        osc, o = av(o, [128, 2, 128], F32)
        scbf, o = av(o, [128, 2, 128], BF16)
        mscol, o = av(o, [128, 16], F32)
        rstdc, o = av(o, [128, 16], F32)
        ms2, o = av(o, [128, 16], F32)
        msn, o = av(o, [128, 16], F32)
        ontm, o = av(o, [128, 2, 128], BF16)
        onstg, o = av(o, [128, 1, 512], BF16)
        sqsc, o = av(o, [128, 128], BF16)
        dbg_lncols_sb, o = av(o, [128, NCH, 8], F32)
        onstg2, o = av(o, [128, 1, 512], BF16)
        assert o <= ARENA, f"phase A {o}"
        a0 = A0B
        xstage, a0 = av(a0, [128, 2, 2048], F32)
        xntm, a0 = av(a0, [128, 2, 2048], BF16)
        sqA0, a0 = av(a0, [128, 2048], F32)
        wg1_sb, a0 = av(a0, [128, KC, 128], BF16)
        rw_sb, a0 = av(a0, [128, KC, 17], BF16)
        o = 0
        xnh, o = av(o, [128, KC, 1024], BF16)
        acc, o = av(o, [128, 8, 2048], F32)
        SWB = o
        hbuf, o = av(o, [128, G, 4, 1024], BF16)
        fgsl, o = av(o, [128, 2, KC, 128], BF16)
        fusl, o = av(o, [128, 2, KC, 128], BF16)
        fdsl, o = av(o, [128, 4, 4, 512], BF16)
        bcst, o = av(o, [128, 2, 512], F32)
        hssc, o = av(o, [128, 2, 512], F32)
        comb_sb, o = av(o, [128, 8, 18], F32)
        rdump_sb, _ = av(SWB, [128, 8, 24], F32)
        wrk_sb, _ = av(SWB + 24 * 1024, [128, 8, 16], F32)
        crow_b = arena[0:1, o:o + 4096].bitcast(F32)
        crow = crow_b.rearrange("p (a b) -> p a b", a=1, b=1024)[:, 0]
        o += 4096
        fdsl4, o = av(o, [128, 1, 4, 512], BF16)
        assert o <= ARENA, f"phase F {o}"
        o = SWB
        wobf, o = av(o, [128, KC, 512], BF16)
        xts, o = av(o, [128, 4, 512], F32)
        assert o <= SWB + 24 * 1024, o - SWB
        o = SWB + 32 * 1024
        onls, o = av(o, [128, 2, KC, 128], BF16)
        outst, o = av(o, [128, 2, 512], F32)
        gsil, o = av(o, [128, 2, 512], F32)
        assert o <= SWB + 48 * 1024, o - SWB

        def program(ctx: Ctx):
            opn, dma = ctx.op, ctx.dma
            TEN, VEC, SCA, GPS, SYN = ENGS

            cst_sync = None
            for nm, dst in (("trig", trig_sb), ("dcols", dcols_sb), ("mask", mask_sb),
                            ("identf", identf_sb), ("identb", identb_sb),
                            ("epsc", epsc_sb), ("nw", nw_sb), ("ones_row", ones_row_sb)):
                cst_sync = dma(SYN, "d_cst", lambda e, nm=nm, dst=dst:
                               e.dma_start(dst[:], w_h[nm]))
            cst_gps = None
            for nm, dst in (("wg1", wg1_sb), ("rw", rw_sb)):
                cst_gps = dma(GPS, "d_cstg", lambda e, nm=nm, dst=dst:
                              e.dma_start(dst[:], w_h[nm]))
            cst_last = cst_sync

            poolP = PPool([P[0], P[1]])
            poolB = PPool([Bb[0], Bb[1]])

            opn(VEC, lambda e: e.memset(cols_sb[:], 0.5))
            opn(SCA, lambda e: e.preload_act_table(AF.Exp))
            opn(SCA, lambda e: e.activation(cols_sb[:, 15:16], cols_sb[:, 14:15],
                                            AF.Exp))
            opn(SCA, lambda e: e.preload_act_table(AF.Silu))
            opn(SCA, lambda e: e.activation(cols_sb[:, 15:16], cols_sb[:, 14:15],
                                            AF.Silu))
            opn(SCA, lambda e: e.preload_act_table(AF.Sqrt))
            opn(SCA, lambda e: e.activation(cols_sb[:, 15:16], cols_sb[:, 14:15],
                                            AF.Sqrt))
            wrm = opn(SCA, lambda e: e.activation(cols_sb[:, 15:16], cols_sb[:, 14:15],
                                                  AF.Sqrt), inc=True)
            opn(SCA, lambda e: e.nop(), waits=[cst_sync])

            spill_tok = [None] * NCH
            xn_evs = [None] * NCH
            xuse = [None] * NCH
            last_rl = last_g1 = None
            vv_par = {}
            xntm_guard = {}
            xt_tok = {}
            sq_tok = {}
            xt_tok[0] = dma(SYN, "d_x0", lambda e: e.dma_start(
                xstage[:, 0], x_h[0:128, :]), waits=[cst_sync])
            sq_tok[0] = opn(SCA, lambda e: e.activation(
                sqA0[:], xstage[:, 0], AF.Square, accum_out=cols_sb[:, 1:2]),
                inc=True, waits=[xt_tok[0]])
            for c in range(NCH):
                sl = c % 2
                cb = sl * 11
                if c + 1 < NCH:
                    sl1 = (c + 1) % 2
                    xt_tok[c + 1] = dma(
                        SYN, f"d_x{sl1}", lambda e, c=c, sl1=sl1:
                        e.dma_start(xstage[:, sl1],
                                    x_h[(c + 1) * 128:(c + 2) * 128, :]),
                        waits=[xuse[c - 1]] if c + 1 >= 2 else [cst_sync])
                rs = opn(VEC, lambda e, sl=sl: e.reduce_sum(
                    cols_sb[:, cb + 0:cb + 1], xstage[:, sl], axis=AX), inc=True,
                    waits=[xt_tok[c]])
                if c + 1 < NCH:
                    cb1 = ((c + 1) % 2) * 11
                    sq_tok[c + 1] = opn(SCA, lambda e, sl1=(c + 1) % 2, cb1=cb1:
                                        e.activation(
                                            sqA0[:], xstage[:, sl1], AF.Square,
                                            accum_out=cols_sb[:, cb1 + 1:cb1 + 2]),
                                        inc=True,
                                        waits=[xt_tok[c + 1],
                                               vv_par.get((c + 1) % 2)])
                t_ = opn(VEC, lambda e: e.tensor_scalar_mul(cols_sb[:, cb + 2:cb + 3],
                                                            cols_sb[:, cb + 0:cb + 1],
                                                            1.0 / D), inc=True,
                         waits=[rs])
                t_ = opn(VEC, lambda e: e.tensor_scalar_mul(cols_sb[:, cb + 3:cb + 4],
                                                            cols_sb[:, cb + 1:cb + 2],
                                                            1.0 / D), inc=True,
                         waits=[sq_tok[c], t_])
                t_ = opn(VEC, lambda e: e.tensor_mul(cols_sb[:, cb + 4:cb + 5], cols_sb[:, cb + 2:cb + 3],
                                                     cols_sb[:, cb + 2:cb + 3]), inc=True,
                         waits=[t_])
                vv = opn(VEC, lambda e: e.tensor_sub(cols_sb[:, cb + 5:cb + 6], cols_sb[:, cb + 3:cb + 4],
                                                     cols_sb[:, cb + 4:cb + 5]), inc=True,
                         waits=[t_])
                vv_par[sl] = vv
                vv2 = opn(VEC, lambda e: e.tensor_scalar_add(cols_sb[:, cb + 9:cb + 10],
                                                             cols_sb[:, cb + 5:cb + 6],
                                                             epsc_sb[:, 0:1]),
                          inc=True, waits=[vv])
                lnt = opn(SCA, lambda e: e.activation(cols_sb[:, cb + 6:cb + 7], cols_sb[:, cb + 9:cb + 10],
                                                      AF.Sqrt), inc=True, waits=[vv2])
                rcp = opn(VEC, lambda e: e.reciprocal(cols_sb[:, cb + 7:cb + 8], cols_sb[:, cb + 6:cb + 7]),
                          inc=True, waits=[lnt])
                t_ = opn(VEC, lambda e: e.tensor_mul(cols_sb[:, cb + 8:cb + 9], cols_sb[:, cb + 7:cb + 8],
                                                     cols_sb[:, cb + 7:cb + 8]), inc=True,
                         waits=[rcp])
                t_ = opn(VEC, lambda e: e.tensor_mul(cols_sb[:, cb + 8:cb + 9], cols_sb[:, cb + 8:cb + 9],
                                                     cols_sb[:, cb + 9:cb + 10]), inc=True,
                         waits=[t_])
                t_ = opn(VEC, lambda e: e.tensor_scalar(cols_sb[:, cb + 8:cb + 9], cols_sb[:, cb + 8:cb + 9],
                                                        -0.5, 1.5, ALU.mult, ALU.add),
                         inc=True, waits=[t_])
                rstd = opn(VEC, lambda e: e.tensor_mul(cols_sb[:, cb + 7:cb + 8], cols_sb[:, cb + 7:cb + 8],
                                                       cols_sb[:, cb + 8:cb + 9]), inc=True,
                          waits=[t_])
                if debug:
                    opn(VEC, lambda e, c=c: e.tensor_copy(
                        dbg_lncols_sb[:, c, :], cols_sb[:, 0:8]), waits=[rstd])
                xnt = opn(VEC, lambda e, sl=sl: e.tensor_scalar(
                    xntm[:, sl], xstage[:, sl], cols_sb[:, cb + 2:cb + 3], cols_sb[:, cb + 7:cb + 8],
                    ALU.subtract, ALU.mult), inc=True,
                    waits=[rstd, xntm_guard.get(sl)])
                xuse[c] = xnt
                evs = []
                for half in range(2):
                    slot, guard, ui = poolB.begin()
                    tr = None
                    for k in range(8):
                        kc = half * 8 + k
                        tr = opn(TEN, lambda e, sl=sl, kc=kc, k=k, slot=slot:
                                 e.transpose(slot[:, k * 128:(k + 1) * 128],
                                             xntm[:, sl, kc * 128:(kc + 1) * 128],
                                             identb_sb[:]),
                                 inc=(k == 7), waits=([xnt, guard] if k == 0 else []))
                    ev = opn(SCA, lambda e, c=c, half=half, slot=slot: e.copy(
                        xn_fm[:, half * 8:(half + 1) * 8, c * 128:(c + 1) * 128],
                        slot[:].rearrange("p (k t) -> p k t", k=8)),
                        inc=True, waits=[tr])
                    poolB.done(ui, ev)
                    evs.append(ev)
                    xntm_guard[sl] = tr
                xn_evs[c] = evs[-1]
                spill_tok[c] = dma(SYN, "d_xs", lambda e, c=c:
                                   e.dma_start(xnS[c], xn_fm[:, :, c * 128:(c + 1) * 128]),
                                   waits=evs)
                slot, guard, ui = poolP.begin()
                mm = None
                for kc in range(KC):
                    mm = opn(TEN, lambda e, kc=kc, c=c, slot=slot: e.matmul(
                        slot[:, 0:17], xn_fm[:, kc, c * 128:(c + 1) * 128],
                        rw_sb[:, kc, :], start=(kc == 0), stop=(kc == KC - 1)),
                        inc=(kc == KC - 1),
                        waits=([guard, cst_gps] + evs if kc == 0 else []))
                last_rl = opn(SCA, lambda e, c=c, slot=slot: e.copy(
                    rlog_sb[:, c, :], slot[:, 0:17]), inc=True, waits=[mm])
                poolP.done(ui, last_rl)
                slot, guard, ui = poolP.begin()
                mm = None
                for kc in range(KC):
                    mm = opn(TEN, lambda e, kc=kc, c=c, slot=slot: e.matmul(
                        slot[:, 0:128], wg1_sb[:, kc, :],
                        xn_fm[:, kc, c * 128:(c + 1) * 128],
                        start=(kc == 0), stop=(kc == KC - 1)),
                        inc=(kc == KC - 1), waits=([guard] if kc == 0 else []))
                last_g1 = opn(SCA, lambda e, c=c, slot=slot: e.copy(
                    g1t_sb[:, c * 128:(c + 1) * 128], slot[:, 0:128]),
                    inc=True, waits=[mm])
                poolP.done(ui, last_g1)

            if debug:
                dma(SYN, "d_y", lambda e: e.dma_start(dbg["xn"], xn_fm[:]),
                    waits=[xn_evs[NCH - 1]])
                dma(SYN, "d_y", lambda e: e.dma_start(dbg["lncols"], dbg_lncols_sb[:]),
                    waits=[("vector", ctx.cnt["vector"])])
                dma(SYN, "d_y", lambda e: e.dma_start(dbg["xst"], xstage[:]),
                    waits=[("vector", ctx.cnt["vector"])])
                dma(SYN, "d_y", lambda e: e.dma_start(dbg["rlog"], rlog_sb[:]),
                    waits=[last_rl])
                dma(SYN, "d_y", lambda e: e.dma_start(dbg["g1t"], g1t_sb[:]),
                    waits=[last_g1])

            opn(SCA, lambda e: e.preload_act_table(AF.Silu))
            if upto == "A0":
                return

            slab_last = {}
            use_n = 0
            vtm_guard = {}
            onstg_guard = {}
            on_sp_last = None
            lastT = {0: None, 1: None}
            lastO = {0: None, 1: None}
            scbf_guard = {0: None, 1: None}
            ontm_guard = {0: None, 1: None}
            qtr_tok = {}
            rot_last = {}
            sbf_tok = {}
            rstdc_guard = [None]
            wsrc = ("wq", "wk", "wv")
            xg_last = None
            gate2_last = None

            for g in range(NG):
                sl_tok = {}

                def load_slab(si, ctx=ctx, g=g, sl_tok=None):
                    pass

                def emit_slab_load(si, g=g):
                    pr, half = si // 2, si % 2
                    un = use_n + si
                    gw = [slab_last.get(un - 3)]
                    assert un < 3 or (un - 3) in slab_last, f"slab guard missing {un}"
                    base = 512 * g + 256 * half
                    sl_tok[si] = dma(GPS, f"d_w{un % 3}",
                                     lambda e, pr=pr, base=base, ss=un % 3:
                                     e.dma_start(wsl[:, ss],
                                                 w_h[wsrc[pr]][:, :, base:base + 256]),
                                     waits=[w for w in gw if w])

                for si in range(3):
                    emit_slab_load(si)
                wg2_tok = dma(GPS, "d_g2", lambda e, g=g: e.dma_start(
                    wg2sl[:, 0], w_h["wg2"][:, 512 * g:512 * (g + 1)]),
                    waits=[gate2_last] if g > 0 else [])

                def pool_last(pool, nslots, slx):
                    idx = [i for i in range(len(pool.cons)) if i % nslots == slx]
                    return pool.cons[idx[-1]] if idx else None

                poolQ = PPool([P[0], P[1], P[2], P[3]])
                poolQ.cons = [pool_last(poolP, 2, 0), pool_last(poolP, 2, 1),
                              lastT[0], lastT[1]]
                poolQ.n = 4
                for pr in range(2):
                    rt_pend = {}

                    def emit_tr(tc, pr=pr, g=g):
                        slot, guard, ui = poolB.begin()
                        trs = None
                        src_base = qtm[:, tc % 3] if pr == 0 else ktm[:, tc]
                        for h in range(G):
                            trs = opn(TEN, lambda e, h=h, slot=slot, src_base=src_base:
                                      e.transpose(slot[:, h * 128:(h + 1) * 128],
                                                  src_base[:, h * 128:(h + 1) * 128],
                                                  identb_sb[:]),
                                      inc=(h == G - 1),
                                      waits=([rt_pend[tc], guard] if h == 0 else []))
                        if pr == 0:
                            qtr_tok[tc] = trs
                        dstf = qfm if pr == 0 else kfm
                        ev = opn(SCA, lambda e, tc=tc, slot=slot, dstf=dstf:
                                 e.copy(
                                     dstf[:, :, tc * 128:(tc + 1) * 128],
                                     slot[:, 0:512].rearrange("p (h t) -> p h t", h=4)),
                                 inc=True, waits=[trs])
                        poolB.done(ui, ev)

                    for tc in range(NCH):
                        rt = None
                        for half in range(2):
                            si = pr * 2 + half
                            un = use_n + si
                            slot, guard, ui = poolQ.begin()
                            ps = slot[:, 0:256]
                            mm = None
                            for kc in range(KC):
                                w0 = []
                                if kc == 0:
                                    w0 = [guard, sl_tok[si]]
                                    if g == 0 and pr == 0 and tc == 0:
                                        w0 += [xn_evs[NCH - 1], spill_tok[NCH - 1]]
                                mm = opn(TEN, lambda e, kc=kc, tc=tc, ss=un % 3, ps=ps:
                                         e.matmul(ps,
                                                  xn_fm[:, kc, tc * 128:(tc + 1) * 128],
                                                  wsl[:, ss, kc, :],
                                                  start=(kc == 0), stop=(kc == KC - 1)),
                                         inc=(kc == KC - 1), waits=w0)
                            if tc == NCH - 1:
                                slab_last[un] = mm
                            ev = opn(SCA, lambda e, ps=ps, half=half: e.activation(
                                rsc[:, half], ps, AF.Silu), inc=True,
                                waits=[mm, rot_last.get(half)])
                            poolQ.done(ui, ev)
                            dstp = qtm[:, tc % 3] if pr == 0 else ktm[:, tc]
                            for hh in range(2):
                                h = half * 2 + hh
                                a_ = rsc[:, half, hh * 128:hh * 128 + 64]
                                b_ = rsc[:, half, hh * 128 + 64:hh * 128 + 128]
                                cs = trig_sb[:, tc, 0:64]
                                sn = trig_sb[:, tc, 64:128]
                                dA = dstp[:, h * 128:h * 128 + 64]
                                dB = dstp[:, h * 128 + 64:h * 128 + 128]
                                t0, t1 = osc[:, 0, 0:64], osc[:, 1, 0:64]
                                ww = [ev] if hh == 0 else []
                                if pr == 0 and hh == 0 and half == 0:
                                    ww.append(qtr_tok.get(tc - 3))
                                opn(VEC, lambda e, a_=a_, cs=cs, t0=t0:
                                    e.tensor_mul(t0, a_, cs),
                                    waits=[w for w in ww if w])
                                e1 = opn(VEC, lambda e, b_=b_, sn=sn, t1=t1:
                                         e.tensor_mul(t1, b_, sn), inc=True)
                                opn(VEC, lambda e, dA=dA, t0=t0, t1=t1:
                                    e.tensor_sub(dA, t0, t1), waits=[e1])
                                opn(VEC, lambda e, a_=a_, sn=sn, t0=t0:
                                    e.tensor_mul(t0, a_, sn))
                                e2 = opn(VEC, lambda e, b_=b_, cs=cs, t1=t1:
                                         e.tensor_mul(t1, b_, cs), inc=True)
                                last = opn(VEC, lambda e, dB=dB, t0=t0, t1=t1:
                                           e.tensor_add(dB, t0, t1),
                                           inc=True, waits=[e2])
                            rot_last[half] = last
                            if half == 1:
                                rt = last
                        if pr == 1:
                            for h in range(G):
                                hg = 4 * g + h
                                rt = opn(VEC, lambda e, tc=tc, h=h, hg=hg:
                                         e.tensor_scalar_mul(
                                             ktm[:, tc, h * 128:(h + 1) * 128],
                                             ktm[:, tc, h * 128:(h + 1) * 128],
                                             dcols_sb[:, H + hg:H + hg + 1]),
                                         inc=(h == G - 1))
                        rt_pend[tc] = rt
                        if tc >= 2:
                            emit_tr(tc - 2)
                        if tc == NCH - 1:
                            emit_tr(NCH - 1 - 1)
                            emit_tr(NCH - 1)
                    if pr == 0:
                        emit_slab_load(3)
                        emit_slab_load(4)
                    else:
                        emit_slab_load(5)
                use_n += 4
                kdk_ready = ("vector", ctx.cnt["vector"])
                poolP = PPool([P[0], P[1]])
                poolP.cons = [pool_last(poolQ, 4, 0), pool_last(poolQ, 4, 1)]
                poolP.n = 2
                lastT[0] = pool_last(poolQ, 4, 2) or lastT[0]
                lastT[1] = pool_last(poolQ, 4, 3) or lastT[1]

                vev = {}
                sup_tok = {}
                msw_batch = []
                for tc in range(NCH):
                    for half in range(2):
                        si = 4 + half
                        un = use_n + half
                        slot, guard, ui = poolP.begin()
                        ps = slot[:, 0:256]
                        mm = None
                        for kc in range(KC):
                            w0 = [guard, sl_tok[si]] if kc == 0 else []
                            mm = opn(TEN, lambda e, kc=kc, tc=tc, ss=un % 3, ps=ps:
                                     e.matmul(ps, xn_fm[:, kc, tc * 128:(tc + 1) * 128],
                                              wsl[:, ss, kc, :],
                                              start=(kc == 0), stop=(kc == KC - 1)),
                                     inc=(kc == KC - 1), waits=[w for w in w0 if w])
                        if tc == NCH - 1:
                            slab_last[un] = mm
                        ww = [mm, vtm_guard.get(tc % VW)]
                        ev = opn(SCA, lambda e, tc=tc, half=half, ps=ps: e.copy(
                            vtm[:, tc % VW, half * 256:(half + 1) * 256], ps),
                            inc=True, waits=[w for w in ww if w])
                        poolP.done(ui, ev)
                        if half == 1:
                            vev[tc] = ev
                    slot, guard, ui = poolP.begin()
                    mm = opn(TEN, lambda e, tc=tc, slot=slot, g=g: e.matmul(
                        slot[:, 0:512], g1t_sb[:, tc * 128:(tc + 1) * 128],
                        wg2sl[:, 0], start=True, stop=True),
                        inc=True, waits=[guard, wg2_tok])
                    gate2_last = mm
                    tnh = opn(SCA, lambda e, slot=slot: e.activation(
                        sg_sb[:], slot[:, 0:512], AF.Tanh, scale=0.5),
                        inc=True, waits=[mm, xg_last])
                    poolP.done(ui, tnh)

                    c = tc
                    fr = {}

                    def front(h, c=c, g=g):
                        hg = 4 * g + h
                        par = h % 2
                        psT = P[2 + par]
                        qt = qfm[:, h, c * 128:(c + 1) * 128]
                        kt = kfm[:, h, c * 128:(c + 1) * 128]
                        w0 = [lastT[par]]
                        if h == 0:
                            w0 += [kdk_ready]
                        mmT = opn(TEN, lambda e, psT=psT, kt=kt, qt=qt: e.matmul(
                            psT[:, 0:128], kt, qt, start=True, stop=True),
                            inc=True, waits=[w for w in w0 if w])
                        scb = opn(VEC, lambda e, psT=psT, par=par:
                                  e.tensor_mul(
                                      scbf[:, par], psT[:, 0:128],
                                      mask_sb[:]), inc=True,
                                  waits=[mmT, scbf_guard[par]])
                        sc_ = (0.5 if hg == 0 else
                               dcols_sb[:, 4 * H + hg:4 * H + hg + 1])
                        sgt = opn(VEC, lambda e, h=h, sc_=sc_: e.tensor_scalar(
                            sg_sb[:, h * 128:(h + 1) * 128],
                            sg_sb[:, h * 128:(h + 1) * 128],
                            sc_, sc_, ALU.mult, ALU.add),
                            inc=True, waits=([tnh] if h == 0 else []))
                        fr[h] = (scb, sgt)

                    def back(h, c=c, g=g):
                        hg = 4 * g + h
                        par = h % 2
                        psT, psO = P[2 + par], P[4 + par]
                        qt = qfm[:, h, c * 128:(c + 1) * 128]
                        vt = vtm[:, c % VW, h * 128:(h + 1) * 128]
                        kdkt = ktm[:, c, h * 128:(h + 1) * 128]
                        scb, sgt = fr[h]
                        msu = opn(TEN, lambda e, psT=psT, kdkt=kdkt, vt=vt: e.matmul(
                            psT[:, 128:256], kdkt, vt, start=True, stop=True),
                            inc=True, waits=[scb, vev[c] if h == 0 else None])
                        wI = [lastO[par]]
                        acc2 = (hg > 0 and c > 0)
                        mi = opn(TEN, lambda e, psO=psO, par=par, vt=vt,
                                 st=not acc2: e.matmul(
                            psO[:, 0:128], scbf[:, par], vt, start=True, stop=st),
                            inc=True, waits=[w for w in wI if w])
                        scbf_guard[par] = mi
                        m2 = mi
                        if c > 0 and hg == 0:
                            m2 = opn(TEN, lambda e, psO=psO, qt=qt, h=h: e.matmul(
                                psO[:, 128:256], qt, Sbf_sb[:, h, :],
                                start=True, stop=True), inc=True,
                                waits=[sbf_tok[h]])
                        elif c > 0:
                            m2 = opn(TEN, lambda e, psO=psO, qt=qt, h=h: e.matmul(
                                psO[:, 0:128], qt, Sbf_sb[:, h, :],
                                start=False, stop=True), inc=True,
                                waits=[sbf_tok[h]])
                        if c == 0:
                            sup = opn(VEC, lambda e, psT=psT, h=h: e.tensor_copy(
                                S_sb[:, h, :], psT[:, 128:256]), inc=True,
                                waits=[msu, sbf_tok.get(h)])
                        else:
                            sup = opn(VEC, lambda e, psT=psT, h=h, hg=hg:
                                      e.scalar_tensor_tensor(
                                          S_sb[:, h, :], S_sb[:, h, :], lamC[hg],
                                          psT[:, 128:256], ALU.mult, ALU.add),
                                      inc=True, waits=[msu, sbf_tok.get(h)])
                        lastT[par] = sup
                        if hg == 0:
                            sbf_tok[h] = opn(SCA, lambda e, h=h: e.copy(
                                Sbf_sb[:, h, :], S_sb[:, h, :]), inc=True,
                                waits=[sup, m2])
                        else:
                            sbf_tok[h] = opn(SCA, lambda e, h=h, hg=hg: e.mul(
                                Sbf_sb[:, h, :], S_sb[:, h, :], lamC[hg]),
                                inc=True, waits=[sup, m2])
                        if hg == 0:
                            if c == 0:
                                mg = opn(VEC, lambda e, psO=psO, hg=hg:
                                         e.tensor_scalar_mul(
                                             osc[:, 0, :], psO[:, 0:128],
                                             dcols_sb[:, 3 * H + hg:3 * H + hg + 1]),
                                         inc=True, waits=[mi])
                            else:
                                mg = opn(VEC, lambda e, psO=psO, hg=hg:
                                         e.tensor_scalar_mul(
                                             osc[:, 1, :], psO[:, 0:128],
                                             dcols_sb[:, 3 * H + hg:3 * H + hg + 1]),
                                         inc=True, waits=[mi, m2])
                                mg = opn(VEC, lambda e, psO=psO, hg=hg:
                                         e.scalar_tensor_tensor(
                                             osc[:, 0, :], psO[:, 128:256],
                                             dcols_sb[:, 2 * H + hg:2 * H + hg + 1],
                                             osc[:, 1, :], ALU.mult, ALU.add),
                                         inc=True, waits=[mg])
                            xg = opn(VEC, lambda e, vt=vt, h=h: e.tensor_mul(
                                vt, osc[:, 0, :], sg_sb[:, h * 128:(h + 1) * 128]),
                                inc=True, waits=[sgt, mg])
                        else:
                            xg = opn(VEC, lambda e, vt=vt, h=h, psO=psO:
                                     e.tensor_mul(
                                         vt, psO[:, 0:128],
                                         sg_sb[:, h * 128:(h + 1) * 128]),
                                     inc=True, waits=[sgt, m2])
                        lastO[par] = xg
                        mst = opn(SCA, lambda e, vt=vt, c=c, h=h: e.activation(
                            sqsc[:], vt, AF.Square,
                            accum_out=mscol[:, (c % 4) * 4 + h:(c % 4) * 4 + h + 1]),
                            inc=True, waits=[xg])
                        msw_batch.append(mst)
                        return xg

                    for h in range(G):
                        front(h)
                        if h >= 1:
                            xg_last = back(h - 1)
                    xg_last = back(G - 1)
                    if c % 4 == 3:
                        msc = opn(VEC, lambda e: e.tensor_scalar(
                            ms2[:], mscol[:], 1.0 / HD, epsc_sb[:, 1:2],
                            ALU.mult, ALU.add), inc=True, waits=[msw_batch[-1]])
                        opn(SCA, lambda e: e.preload_act_table(AF.Sqrt))
                        lnt2 = opn(SCA, lambda e: e.activation(
                            rstdc[:], ms2[:], AF.Sqrt), inc=True,
                            waits=[msc, rstdc_guard[0]])
                        rcp2 = opn(VEC, lambda e: e.reciprocal(rstdc[:], rstdc[:]),
                                   inc=True, waits=[lnt2])
                        t2_ = opn(VEC, lambda e: e.tensor_mul(msn[:], rstdc[:],
                                                              rstdc[:]), inc=True,
                                  waits=[rcp2])
                        t2_ = opn(VEC, lambda e: e.tensor_mul(msn[:], msn[:], ms2[:]),
                                  inc=True, waits=[t2_])
                        t2_ = opn(VEC, lambda e: e.tensor_scalar(
                            msn[:], msn[:], -0.5, 1.5, ALU.mult, ALU.add),
                            inc=True, waits=[t2_])
                        rst = opn(VEC, lambda e: e.tensor_mul(rstdc[:], rstdc[:],
                                                              msn[:]), inc=True,
                                  waits=[t2_])
                        opn(SCA, lambda e: e.preload_act_table(AF.Silu))
                        msw_batch = []
                        ntoks = {}
                        for cc in range(c - 3, c + 1):
                            for h in range(G):
                                col = (cc % 4) * 4 + h
                                ww = [rst if (h == 0 and cc == c - 3) else None]
                                ntoks[(cc, h)] = opn(
                                    VEC, lambda e, cc=cc, h=h, col=col:
                                    e.tensor_scalar_mul(
                                        vtm[:, cc % VW, h * 128:(h + 1) * 128],
                                        vtm[:, cc % VW, h * 128:(h + 1) * 128],
                                        rstdc[:, col:col + 1]),
                                    inc=True, waits=[w for w in ww if w])
                        for cc in range(c - 3, c + 1):
                            slot, guard, ui = poolB.begin()
                            trs = None
                            for h in range(G):
                                trs = opn(TEN, lambda e, cc=cc, h=h, slot=slot:
                                          e.transpose(
                                              slot[:, h * 128:(h + 1) * 128],
                                              vtm[:, cc % VW,
                                                  h * 128:(h + 1) * 128],
                                              identb_sb[:]), inc=(h == G - 1),
                                          waits=([ntoks[(cc, G - 1)], guard]
                                                 if h == 0 else []))
                            vtm_guard[cc % VW] = trs
                            rstdc_guard[0] = ntoks[(c, G - 1)]
                            osl = cc % 2
                            obuf = onstg if osl == 0 else onstg2
                            ev = None
                            for h in range(G):
                                hg = 4 * g + h
                                ev = opn(SCA, lambda e, slot=slot, h=h, hg=hg,
                                         obuf=obuf:
                                         e.mul(obuf[:, 0, h * 128:(h + 1) * 128],
                                               slot[:, h * 128:(h + 1) * 128],
                                               nw_sb[:, hg:hg + 1]),
                                         inc=(h == G - 1),
                                         waits=([trs, onstg_guard.get(osl)]
                                                if h == 0 else []))
                            poolB.done(ui, ev)
                            spt = dma(SYN, "d_on" if osl == 0 else "d_on1",
                                      lambda e, cc=cc, g=g, obuf=obuf: e.dma_start(
                                          onS[cc, :, 4 * g:4 * (g + 1), :],
                                          obuf[:, 0].rearrange("p (h t) -> p h t",
                                                               h=4)),
                                      waits=[ev])
                            onstg_guard[osl] = spt
                            on_sp_last = spt
                use_n += 2

            if debug:
                jn = [(e2, ctx.cnt[e2]) for e2 in ("tensor", "vector", "scalar")]
                dma(SYN, "d_y", lambda e: e.dma_start(dbg["qf"], qfm[:]), waits=jn)
                dma(SYN, "d_y", lambda e: e.dma_start(dbg["kf"], kfm[:]))
                dma(SYN, "d_y", lambda e: e.dma_start(dbg["qtm"], qtm[:]))
                dma(SYN, "d_y", lambda e: e.dma_start(dbg["rsc"], rsc[:]))
                dma(SYN, "d_y", lambda e: e.dma_start(dbg["vt"], vtm[:]))
                dma(SYN, "d_y", lambda e: e.dma_start(dbg["ktm"], ktm[:]))
                dma(SYN, "d_y", lambda e: e.dma_start(dbg["on"], onS[:]),
                    waits=[on_sp_last])
            dump_gate = [("dma", "d_y", ctx.dcnt["d_y"])] if debug else []

            if upto == "A":
                return
            joinA = [(e2, ctx.cnt[e2]) for e2 in ("tensor", "vector", "scalar")]
            poolF = PPool([P[0], P[1], P[2], P[3]])
            poolD = PPool([P[4], P[5],
                           Bb[0].bitcast(F32), Bb[1].bitcast(F32)])
            groups = [[0, 1, 2, 3], [4, 5, 6, 7], [8, 9, 10, 11], [12, 13, 14, 15], [16]]
            fg_guard = {}
            fu_guard = {}
            fd_ptr = [0]
            fd_slot_tok = {}

            def fd_ap(sl5):
                return fdsl[:, sl5] if sl5 < 4 else fdsl4[:, 0]
            hssc_guard = {}
            bcst_guard = {}
            crow_guard = None
            y_last = None
            sweep_done = None
            router_gate = None

            for half in range(2):
                hw = joinA + [on_sp_last, spill_tok[NCH - 1]] if half == 0 \
                    else [sweep_done]
                rtw = hw + dump_gate if half == 0 else router_gate
                xh = dma(SYN, "d_xh", lambda e, half=half: e.dma_start(
                    xnh[:], xnS[half * 8:(half + 1) * 8].transpose([1, 2, 0, 3])),
                    waits=hw)
                opn(SCA, lambda e: e.preload_act_table(AF.Exp))
                sme = opn(SCA, lambda e, half=half: e.activation(
                    comb_sb[:, :, 0:16], rlog_sb[:, half * 8:(half + 1) * 8, 0:16],
                    AF.Exp), inc=True, waits=rtw)
                th = opn(SCA, lambda e, half=half: e.activation(
                    comb_sb[:, :, 16:17], rlog_sb[:, half * 8:(half + 1) * 8, 16:17],
                    AF.Tanh, scale=0.5), inc=True, waits=[sme])
                shg_t = opn(VEC, lambda e: e.tensor_scalar(
                    comb_sb[:, :, 16:17], comb_sb[:, :, 16:17],
                    0.5, 0.5, ALU.mult, ALU.add), inc=True, waits=[th])
                tok_c = {}
                for tcl in range(8):
                    tok_c[tcl] = opn(VEC, lambda e, tcl=tcl: e.tensor_copy(
                        wrk_sb[:, tcl, :], comb_sb[:, tcl, 0:16]),
                        inc=True, waits=[sme] if tcl == 0 else [])
                for it in range(3):
                    mx_t = {}
                    for tcl in range(8):
                        mx_t[tcl] = opn(VEC, lambda e, tcl=tcl: e.reduce_max(
                            cols_sb[:, tcl:tcl + 1], wrk_sb[:, tcl, :], axis=AX),
                            inc=True, waits=[tok_c[tcl]])
                    for tcl in range(8):
                        tok_c[tcl] = opn(VEC, lambda e, tcl=tcl:
                                         e.scalar_tensor_tensor(
                                             wrk_sb[:, tcl, :], wrk_sb[:, tcl, :],
                                             cols_sb[:, tcl:tcl + 1],
                                             wrk_sb[:, tcl, :],
                                             ALU.is_lt, ALU.mult),
                                         inc=True, waits=[mx_t[tcl]])
                m4_t = {}
                for tcl in range(8):
                    m4_t[tcl] = opn(VEC, lambda e, tcl=tcl: e.reduce_max(
                        cols_sb[:, tcl:tcl + 1], wrk_sb[:, tcl, :], axis=AX),
                        inc=True, waits=[tok_c[tcl]])
                sg_t = {}
                for tcl in range(8):
                    sg_t[tcl] = opn(VEC, lambda e, tcl=tcl: e.scalar_tensor_tensor(
                        comb_sb[:, tcl, 0:16], comb_sb[:, tcl, 0:16],
                        cols_sb[:, tcl:tcl + 1], comb_sb[:, tcl, 0:16],
                        ALU.is_ge, ALU.mult), inc=True, waits=[m4_t[tcl]])
                ss_t = {}
                for tcl in range(8):
                    ss_t[tcl] = opn(VEC, lambda e, tcl=tcl: e.reduce_sum(
                        cols_sb[:, 8 + tcl:9 + tcl], comb_sb[:, tcl, 0:16], axis=AX),
                        inc=True, waits=[sg_t[tcl]])
                rc_t = {}
                for tcl in range(8):
                    rc_t[tcl] = opn(VEC, lambda e, tcl=tcl: e.reciprocal(
                        cols_sb[:, 16 + tcl:17 + tcl], cols_sb[:, 8 + tcl:9 + tcl]),
                        inc=True, waits=[ss_t[tcl]])
                comb_done = {}
                for tcl in range(8):
                    comb_done[tcl] = opn(VEC, lambda e, tcl=tcl: e.tensor_scalar_mul(
                        comb_sb[:, tcl, 0:16], comb_sb[:, tcl, 0:16],
                        cols_sb[:, 16 + tcl:17 + tcl]), inc=True,
                        waits=[rc_t[tcl]] + ([shg_t] if tcl == 0 else []))
                if debug:
                    dma(SYN, "d_y", lambda e, half=half: e.dma_start(
                        dbg["comb"][:, half * 8:(half + 1) * 8, :],
                        comb_sb[:]), waits=[comb_done[7]])
                opn(SCA, lambda e: e.preload_act_table(AF.Silu))

                acc_tok = {}
                hb_guard = {}
                sw_state = {"wo_guard": {}, "onl_guard": {}, "xt_guard": {},
                            "out_guard": {}, "gs_guard": {}, "wcv": {},
                            "last_t2": None, "mms_last": None}

                def emit_wo_stream(jq, half=half):
                    gw = [sw_state["mms_last"]]
                    if jq == 0:
                        gw += [hb_guard.get((0, 0, 0)), hb_guard.get((1, 0, 0))]
                    toks = []
                    for kc in range(KC):
                        toks.append(dma(GPS, f"d_wo{kc}", lambda e, jq=jq, kc=kc:
                                    e.dma_start(wobf[:, kc, :], w_h["wo"][jq, kc]),
                                    waits=[w for w in gw if w] if kc == 0 else []))
                    sw_state["wcv"][jq] = toks

                def emit_sweep(jq, half=half):
                    nonlocal y_last
                    for tcl in range(8):
                        cg = half * 8 + tcl
                        ts_ = tcl % 2
                        gw = [sw_state["onl_guard"].get(ts_)]
                        if jq == 0 and tcl < 2:
                            gw.append(fg_guard.get(tcl % 2))
                            gw.append(fu_guard.get(tcl % 2))
                            gw.append(("dma", "d_on", ctx.dcnt["d_on"]))
                            gw.append(("dma", "d_on1", ctx.dcnt["d_on1"]))
                        ot = dma(SYN, f"d_onl{ts_}", lambda e, cg=cg, ts_=ts_:
                                 e.dma_start(onls[:, ts_],
                                             onS[cg]),
                                 waits=[w for w in gw if w])
                        ts4 = tcl % 4
                        gw = [sw_state["xt_guard"].get(ts4)]
                        if jq == 0 and tcl < 4:
                            gw.append(hb_guard.get((2, 0, 0)))
                        xt = dma(SYN, f"d_xt{ts4}", lambda e, cg=cg, jq=jq, ts4=ts4:
                                 e.dma_start(xts[:, ts4],
                                             x_h[cg * 128:(cg + 1) * 128,
                                                 jq * 512:(jq + 1) * 512]),
                                 waits=[w for w in gw if w])
                        slot, guard, ui = poolF.begin()
                        mms = None
                        for kc in range(KC):
                            w0 = [guard, ot] if kc == 0 else []
                            if tcl == 0:
                                w0 = w0 + [sw_state["wcv"][jq][kc]]
                            mms = opn(TEN, lambda e, kc=kc, ts_=ts_, slot=slot:
                                      e.matmul(slot[:, 0:512], onls[:, ts_, kc, :],
                                               wobf[:, kc, :],
                                               start=(kc == 0), stop=(kc == KC - 1)),
                                      inc=(kc == KC - 1), waits=[w for w in w0 if w])
                        sw_state["onl_guard"][ts_] = mms
                        sw_state["mms_last"] = mms
                        gs = opn(SCA, lambda e, tcl=tcl, jq=jq, ts_=ts_: e.activation(
                            gsil[:, ts_], acc[:, tcl, jq * 512:(jq + 1) * 512],
                            AF.Silu), inc=True,
                            waits=[acc_tok[(tcl, jq)], sw_state["gs_guard"].get(ts_)])
                        t1 = opn(VEC, lambda e, ts_=ts_, slot=slot: e.tensor_mul(
                            outst[:, ts_], slot[:, 0:512], gsil[:, ts_]),
                            inc=True, waits=[mms, gs, sw_state["out_guard"].get(ts_)])
                        poolF.done(ui, t1)
                        sw_state["gs_guard"][ts_] = t1
                        t2 = opn(VEC, lambda e, ts_=ts_, ts4=ts4: e.tensor_add(
                            outst[:, ts_], outst[:, ts_], xts[:, ts4]),
                            inc=True, waits=[xt])
                        sw_state["xt_guard"][ts4] = t2
                        yw = dma(GPS, "d_yg", lambda e, cg=cg, jq=jq, ts_=ts_:
                                 e.dma_start(y_h[cg * 128:(cg + 1) * 128,
                                                 jq * 512:(jq + 1) * 512],
                                             outst[:, ts_]), waits=[t2])
                        sw_state["out_guard"][ts_] = yw
                        sw_state["last_t2"] = t2
                        y_last = yw

                for gi, grp in enumerate(groups):
                    hready = {}
                    fdt_pre = {}
                    for ei, ex in enumerate(grp):
                        evc = None
                        for hr in range(2):
                            slot, guard, ui = poolF.begin()
                            trc = None
                            for tt in range(4):
                                tcl = hr * 4 + tt
                                col = 16 if ex == 16 else ex
                                trc = opn(TEN, lambda e, tt=tt, tcl=tcl, col=col,
                                          slot=slot:
                                          e.transpose(slot[0:1, tt * 128:tt * 128 + 128],
                                                      comb_sb[:, tcl, col:col + 1],
                                                      identf_sb[:]),
                                          inc=(tt == 3),
                                          waits=([guard, comb_done[7], crow_guard]
                                                 if tt == 0 else []))
                            evc = opn(VEC, lambda e, hr=hr, slot=slot: e.tensor_copy(
                                crow[0:1, hr * 512:(hr + 1) * 512], slot[0:1, 0:512]),
                                inc=True, waits=[trc])
                            poolF.done(ui, evc)
                        bc_tok = [None, None]
                        for nt in range(2):
                            slot, guard, ui = poolF.begin()
                            mmb = opn(TEN, lambda e, nt=nt, slot=slot: e.matmul(
                                slot[:, 0:512], ones_row_sb[:],
                                crow[0:1, nt * 512:(nt + 1) * 512],
                                start=True, stop=True), inc=True, waits=[guard, evc])
                            if nt == 1:
                                crow_guard = mmb
                            bev = opn(SCA, lambda e, nt=nt, slot=slot: e.copy(
                                bcst[:, nt], slot[:, 0:512]), inc=True, waits=[mmb])
                            poolF.done(ui, bev)
                            bc_tok[nt] = bev
                        hlast = None
                        for mf in range(4):
                            gsl = (ex * 4 + mf) % 2 if ex != 16 else mf % 2
                            gidx = 4 * (ex if ex != 16 else 0) + mf
                            wsrc_g = (w_h["weg"][ex, mf] if ex != 16
                                      else w_h["wsg"][mf])
                            wsrc_u = (w_h["weu"][ex, mf] if ex != 16
                                      else w_h["wsu"][mf])
                            fgt = dma(GPS, f"d_fg{gsl}", lambda e, w=wsrc_g, gsl=gsl:
                                      e.dma_start(fgsl[:, gsl], w),
                                      waits=[fg_guard.get(gsl),
                                             xh if (gi == 0 and mf == 0) else None])
                            fut = dma(GPS, f"d_fu{gsl}", lambda e, w=wsrc_u, gsl=gsl:
                                      e.dma_start(fusl[:, gsl], w),
                                      waits=[fu_guard.get(gsl)])
                            for nt in range(2):
                                slotg, guardg, uig = poolF.begin()
                                mmg = None
                                for kc in range(KC):
                                    w0 = ([guardg, fgt, xh] if kc == 0 else [])
                                    if kc == 0:
                                        w0.append(hb_guard.get((3 if gi == 4 else ei,
                                                                mf, nt)))
                                    mmg = opn(TEN, lambda e, kc=kc, nt=nt, gsl=gsl,
                                              slotg=slotg: e.matmul(
                                                  slotg[:, 0:512],
                                                  fgsl[:, gsl, kc, :],
                                                  xnh[:, kc, nt * 512:(nt + 1) * 512],
                                                  start=(kc == 0), stop=(kc == KC - 1)),
                                              inc=(kc == KC - 1),
                                              waits=[w for w in w0 if w])
                                slotu, guardu, uiu = poolF.begin()
                                mmu = None
                                for kc in range(KC):
                                    w0 = [guardu, fut] if kc == 0 else []
                                    mmu = opn(TEN, lambda e, kc=kc, nt=nt, gsl=gsl,
                                              slotu=slotu: e.matmul(
                                                  slotu[:, 0:512],
                                                  fusl[:, gsl, kc, :],
                                                  xnh[:, kc, nt * 512:(nt + 1) * 512],
                                                  start=(kc == 0), stop=(kc == KC - 1)),
                                              inc=(kc == KC - 1),
                                              waits=[w for w in w0 if w])
                                if mf == 3 and nt == 1:
                                    fg_guard[gsl] = mmg
                                    fu_guard[gsl] = mmu
                                else:
                                    fg_guard[gsl] = mmg
                                    fu_guard[gsl] = mmu
                                hs_ = opn(SCA, lambda e, nt=nt, slotg=slotg:
                                          e.activation(hssc[:, nt], slotg[:, 0:512],
                                                       AF.Silu), inc=True,
                                          waits=[mmg, hssc_guard.get(nt)])
                                poolF.done(uig, hs_)
                                h1 = opn(VEC, lambda e, nt=nt, slotu=slotu:
                                         e.tensor_mul(hssc[:, nt], hssc[:, nt],
                                                      slotu[:, 0:512]),
                                         inc=True, waits=[hs_, mmu])
                                poolF.done(uiu, h1)
                                hsl = 3 if gi == 4 else ei
                                hlast = opn(VEC, lambda e, hsl=hsl, mf=mf, nt=nt:
                                            e.tensor_mul(
                                                hbuf[:, hsl, mf,
                                                     nt * 512:(nt + 1) * 512],
                                                hssc[:, nt], bcst[:, nt]),
                                            inc=True, waits=[bc_tok[nt]])
                                hssc_guard[nt] = hlast
                                bcst_guard[nt] = hlast
                        hready[ei] = hlast
                    fd_pre_slot = {}
                    for ei, ex in enumerate(grp):
                        sl5 = fd_ptr[0] % 5
                        fd_ptr[0] += 1
                        fd_pre_slot[ei] = sl5
                        wsrc_d0 = (w_h["wed"][ex, 0] if ex != 16 else w_h["wsd"][0])
                        fdt_pre[ei] = dma(GPS, f"d_fd{sl5}",
                                          lambda e, w=wsrc_d0, sl5=sl5:
                                          e.dma_start(fd_ap(sl5), w),
                                          waits=[fd_slot_tok.get(sl5)])
                    if gi == 4:
                        emit_wo_stream(0)
                    fdt_nxt = {}
                    slot_nxt = {}
                    for dq in range(4):
                        if dq == 0:
                            fdt = fdt_pre
                            dq_slot = fd_pre_slot
                        else:
                            fdt = fdt_nxt
                            dq_slot = slot_nxt
                        for tcl in range(8):
                            slot, guard, ui = poolD.begin()
                            mmd = None
                            for ei, ex in enumerate(grp):
                                hsl = 3 if gi == 4 else ei
                                for kf in range(4):
                                    w0 = []
                                    if kf == 0:
                                        w0 = [hready[ei], fdt[ei]]
                                        if ei == 0:
                                            w0.append(guard)
                                    mmd = opn(TEN, lambda e, hsl=hsl, kf=kf, tcl=tcl,
                                              dq=dq, slot=slot, ei=ei,
                                              sl5=dq_slot[ei],
                                              ng=len(grp): e.matmul(
                                                  slot[:, 0:512],
                                                  hbuf[:, hsl, kf,
                                                       tcl * 128:(tcl + 1) * 128],
                                                  fd_ap(sl5)[:, kf, :],
                                                  start=(ei == 0 and kf == 0),
                                                  stop=(ei == ng - 1 and kf == 3)),
                                              inc=(ei == len(grp) - 1 and kf == 3),
                                              waits=[w for w in w0 if w])
                            if tcl == 7:
                                for ei in range(len(grp)):
                                    fd_slot_tok[dq_slot[ei]] = mmd
                            at = None
                            tgt = acc[:, tcl, dq * 512:(dq + 1) * 512]
                            if gi == 0:
                                at = opn(VEC, lambda e, tgt=tgt, slot=slot:
                                         e.tensor_copy(tgt, slot[:, 0:512]),
                                         inc=True,
                                         waits=[mmd, acc_tok.get((tcl, dq))])
                            else:
                                at = opn(VEC, lambda e, tgt=tgt, slot=slot:
                                         e.tensor_add(tgt, tgt, slot[:, 0:512]),
                                         inc=True, waits=[mmd])
                            poolD.done(ui, at)
                            acc_tok[(tcl, dq)] = at
                        if dq < 3:
                            fdt_nxt = {}
                            slot_nxt = {}
                            for ei, ex in enumerate(grp):
                                sl5 = fd_ptr[0] % 5
                                fd_ptr[0] += 1
                                slot_nxt[ei] = sl5
                                wsrc_d = (w_h["wed"][ex, dq + 1] if ex != 16
                                          else w_h["wsd"][dq + 1])
                                fdt_nxt[ei] = dma(GPS, f"d_fd{sl5}",
                                                  lambda e, w=wsrc_d, sl5=sl5:
                                                  e.dma_start(fd_ap(sl5), w),
                                                  waits=[fd_slot_tok.get(sl5)])
                        if gi == 4:
                            emit_sweep(dq)
                            if dq < 3:
                                emit_wo_stream(dq + 1)
                    dtok = acc_tok[(7, 3)]
                    for ei in range(4):
                        for mf in range(4):
                            for nt in range(2):
                                hb_guard[(ei, mf, nt)] = dtok
                if debug:
                    dma(SYN, "d_y", lambda e, half=half: e.dma_start(
                        dbg["acc"][half], acc[:]), waits=[acc_tok[(7, 3)]])

                sweep_done = sw_state["last_t2"]
                router_gate = [acc_tok[(7, 3)], evc, bc_tok[1]]

            opn(SYN, lambda e: e.nop(), waits=[y_last])

        for role in ENGS:
            def mk(r):
                def _f(eng):
                    program(Ctx(r, eng, sems, dsems))
                return _f
            getattr(block, role)(mk(role))

    return nc, dbg


_NC_CACHE = {}


def kernel(**inputs):
    key = "main"
    if key not in _NC_CACHE:
        _NC_CACHE[key] = build_program(debug=False)
    nc, _ = _NC_CACHE[key]
    wl = host_relayout(inputs)
    cst = host_constants()
    x = np.asarray(inputs["L0_attn_x"], np.float32)
    base = dict(wl)
    base.update(cst)
    in_maps = []
    for core in range(B):
        m = dict(base)
        m["x"] = np.ascontiguousarray(x[core])
        in_maps.append(m)
    res = run_bass_kernel_spmd(nc, in_maps, core_ids=list(range(B)))
    out = np.empty((B, T, D), np.float32)
    for core in range(B):
        out[core] = res.results[core]["y"]
    return out


if __name__ == "__main__":
    nc, _ = build_program(debug=True)
    print("built", nc.n_instructions())
